# version 08
# baseline (speedup 1.0000x reference)
import contextlib
import numpy as np

import concourse.bass as bass
import concourse.mybir as mybir
from concourse.bass_utils import run_bass_kernel_spmd
from concourse.alu_op_type import AluOpType

F32 = mybir.dt.float32
F32R = mybir.dt.float32r
BF16 = mybir.dt.bfloat16
ACTF = mybir.ActivationFunctionType
OP = AluOpType
AXX = mybir.AxisListType.X

B = 8
L = 4096
DM = 1024
DI = 2048
NST = 16
RNK = 64
K = 4
NG = DI // 128
MG = DM // 128
TC = 256
NTC = L // TC
TSUB = TC // 128
NQ = 4
NU = NST // NQ
TP2 = TC + 2
CW = TC + 6


class Tok:
    def __init__(self, stack, nc, name):
        self.h = stack.enter_context(nc.semaphore(name))
        self.n = 0

    def take(self, k=1):
        self.n += k
        return self.n


DEBUG_STASH = False

def build_program():
    nc = bass.Bass("TRN2", target_bir_lowering=False, debug=False)

    d_u = nc.dram_tensor("u", [L, DM], F32, kind="ExternalInput").ap()
    d_WinT = nc.dram_tensor("WinT", [DM, 2 * DI], F32, kind="ExternalInput").ap()
    d_WoutT = nc.dram_tensor("WoutT", [DI, DM], F32, kind="ExternalInput").ap()
    d_WfT = nc.dram_tensor("WfT", [DM, DM], F32, kind="ExternalInput").ap()
    d_xpT = {b: nc.dram_tensor(f"xpT_{b}", [128, NG * 96], F32, kind="ExternalInput").ap() for b in "fb"}
    d_dtwT = {b: nc.dram_tensor(f"dtwT_{b}", [RNK, DI], F32, kind="ExternalInput").ap() for b in "fb"}
    d_Alog = {b: nc.dram_tensor(f"Alog_{b}", [128, NG * NST], F32, kind="ExternalInput").ap() for b in "fb"}
    d_convw = {b: nc.dram_tensor(f"convw_{b}", [128, NG * K], F32, kind="ExternalInput").ap() for b in "fb"}
    d_convb = {b: nc.dram_tensor(f"convb_{b}", [128, NG], F32, kind="ExternalInput").ap() for b in "fb"}
    d_dtb = {b: nc.dram_tensor(f"dtb_{b}", [128, NG], F32, kind="ExternalInput").ap() for b in "fb"}
    d_D = {b: nc.dram_tensor(f"Dv_{b}", [128, NG], F32, kind="ExternalInput").ap() for b in "fb"}
    d_out = nc.dram_tensor("out", [L, DM], F32, kind="ExternalOutput").ap()

    _skind = "ExternalOutput" if DEBUG_STASH else "Internal"
    d_dbgw = nc.dram_tensor("dbgw", [128, MG * 2 * DI], BF16, kind="ExternalOutput").ap() if DEBUG_STASH else None
    s_xin = nc.dram_tensor("s_xin", [DI, L], F32, kind=_skind).ap()
    s_siluz = nc.dram_tensor("s_siluz", [DI, L], BF16, kind=_skind).ap()
    s_yfwd = nc.dram_tensor("s_yfwd", [DI, L], BF16, kind=_skind).ap()
    s_bc = nc.dram_tensor("s_bc", [32, L], BF16, kind=_skind).ap()
    d_dbgxc = nc.dram_tensor("dbgxc", [DI, L], F32, kind=_skind).ap() if DEBUG_STASH else None
    d_dbgdt = nc.dram_tensor("dbgdt", [DI, L], F32, kind=_skind).ap() if DEBUG_STASH else None
    d_dbgyb = nc.dram_tensor("dbgyb", [DI, L], BF16, kind=_skind).ap() if DEBUG_STASH else None
    d_dbgx1 = nc.dram_tensor("dbgx1", [L, DM], F32, kind=_skind).ap() if DEBUG_STASH else None
    d_dbgat = nc.dram_tensor("dbgat", [L, DM], BF16, kind=_skind).ap() if DEBUG_STASH else None

    ident_src = nc.inline_tensor(np.eye(128, dtype=np.float32), name="ident_src")
    _lc = np.zeros((128, 24), np.float32); _lc[:, 4:12] = 1.0
    lncol_src = nc.inline_tensor(_lc, name="lncol_src")
    eps_src = nc.inline_tensor(np.full((128, 1), 1e-5, np.float32), name="eps_src")

    prog = {"sync": [], "gpsimd": [], "vector": [], "scalar": [], "tensor": []}

    def w(eng, tok, val):
        if val and val > 0:
            prog[eng].append(lambda e, t=tok, v=val: e.wait_ge(t.h, v))

    def stage(eng, fn, waits=(), inc=None, inck=1):
        for (t, v) in waits:
            w(eng, t, v)
        val = inc.take(inck) if inc is not None else None
        def wrapped(e, fn=fn, inc=inc, inck=inck):
            last = fn(e)
            if inc is not None:
                last.then_inc(inc.h, inck)
        prog[eng].append(wrapped)
        return val

    with contextlib.ExitStack() as ST:
        E = ST.enter_context

        s_ld = Tok(ST, nc, "s_ld")
        s_st = Tok(ST, nc, "s_st")
        s_act = Tok(ST, nc, "s_act")
        s_dve = Tok(ST, nc, "s_dve")
        s_pe = Tok(ST, nc, "s_pe")
        s_out = Tok(ST, nc, "s_out")
        s_gp = Tok(ST, nc, "s_gp")
        s_ph = Tok(ST, nc, "s_ph")

        ident = E(nc.sbuf_tensor("ident", [128, 128], BF16))
        ident32 = E(nc.sbuf_tensor("ident32", [128, 128], F32))
        xpT = {b: E(nc.sbuf_tensor(f"xpT{b}", [128, NG * 96], F32)) for b in "fb"}
        dtwT = {b: E(nc.sbuf_tensor(f"dtwT{b}", [RNK, DI], F32R)) for b in "fb"}
        Atile = {b: E(nc.sbuf_tensor(f"At{b}", [128, NG * NST], F32)) for b in "fb"}
        convw = {b: E(nc.sbuf_tensor(f"cw{b}", [128, NG * K], F32)) for b in "fb"}
        convb = {b: E(nc.sbuf_tensor(f"cb{b}", [128, NG], F32)) for b in "fb"}
        dtb = {b: E(nc.sbuf_tensor(f"dtb{b}", [128, NG], F32)) for b in "fb"}
        Dv = {b: E(nc.sbuf_tensor(f"Dv{b}", [128, NG], F32)) for b in "fb"}
        carry = {b: E(nc.sbuf_tensor(f"carry{b}", [128, NG * NST], F32)) for b in "fb"}

        xsl = E(nc.sbuf_tensor("xsl", [128, NG * CW], F32))
        dtsl = E(nc.sbuf_tensor("dtsl", [128, NG * TC], F32))
        szsl = E(nc.sbuf_tensor("szsl", [128, NG * TC], BF16))
        Bsl = E(nc.sbuf_tensor("Bsl", [128, NST * TC], BF16))
        Csl = E(nc.sbuf_tensor("Csl", [128, NST * TC], BF16))
        dtpre = E(nc.sbuf_tensor("dtpre", [RNK, TC], F32R))
        bcrow = E(nc.sbuf_tensor("bcrow", [128, TC], BF16))
        wt = E(nc.sbuf_tensor("wt", [128, TC], F32))
        conv_acc = E(nc.sbuf_tensor("conv_acc", [128, TC], F32))
        halo = E(nc.sbuf_tensor("halo", [128, NG * 3], F32))
        epsb = E(nc.sbuf_tensor("epsb", [128, 1], F32))
        lncol0 = E(nc.sbuf_tensor("lncol0", [128, 24], F32))
        etmp = E(nc.sbuf_tensor("etmp", [128, TC], F32))
        dasl = [E(nc.sbuf_tensor(f"dasl{i}", [128, NQ * TP2], F32)) for i in range(2)]
        usl = [E(nc.sbuf_tensor(f"usl{i}", [128, NQ * TP2], F32)) for i in range(2)]
        hsl = [E(nc.sbuf_tensor(f"hsl{i}", [128, NQ * TP2], F32)) for i in range(2)]
        htsl = [E(nc.sbuf_tensor(f"htsl{i}", [128, NQ * TC], BF16)) for i in range(2)]
        ysl = [E(nc.sbuf_tensor(f"ysl{i}", [128, TC], BF16)) for i in range(2)]

        pb = [E(nc.psum_tensor(f"pb{i}", [128, 512], F32)) for i in range(8)]

        tcst = 0
        for b in "fb":
            tcst = stage("sync", lambda e, b=b: e.dma_start(out=Atile[b][:], in_=d_Alog[b]), inc=s_ld, inck=16)
            tcst = stage("sync", lambda e, b=b: e.dma_start(out=convw[b][:], in_=d_convw[b]), inc=s_ld, inck=16)
            tcst = stage("sync", lambda e, b=b: e.dma_start(out=convb[b][:], in_=d_convb[b]), inc=s_ld, inck=16)
            tcst = stage("sync", lambda e, b=b: e.dma_start(out=dtb[b][:], in_=d_dtb[b]), inc=s_ld, inck=16)
            tcst = stage("sync", lambda e, b=b: e.dma_start(out=Dv[b][:], in_=d_D[b]), inc=s_ld, inck=16)
            tcst = stage("sync", lambda e, b=b: e.dma_start(out=xpT[b][:], in_=d_xpT[b]), inc=s_ld, inck=16)
        tid = stage("sync", lambda e: e.dma_start(out=dtsl[:, 0:128], in_=ident_src.ap()), inc=s_ld, inck=16)
        t_lc = stage("sync", lambda e: e.dma_start(out=lncol0[:], in_=lncol_src.ap()), inc=s_ld, inck=16)
        t_eps = stage("sync", lambda e: e.dma_start(out=epsb[:], in_=eps_src.ap()), inc=s_ld, inck=16)

        tid32 = stage("sync", lambda e: e.dma_start(out=ident32[:], in_=ident_src.ap()), inc=s_ld, inck=16)
        t_ident = stage("vector", lambda e: e.tensor_copy(out=ident[:], in_=dtsl[:, 0:128]),
                        waits=[(s_ld, tid)], inc=s_dve)

        t_round = t_ident
        for b in "fb":
            t1 = stage("sync",
                       lambda e, b=b: e.dma_start(out=dtsl[0:RNK, 0:DI], in_=d_dtwT[b]),
                       waits=[(s_dve, t_round)], inc=s_ld, inck=16)
            t_round = stage("vector",
                            lambda e, b=b: e.tensor_copy(out=dtwT[b][:], in_=dtsl[0:RNK, 0:DI]),
                            waits=[(s_ld, t1)], inc=s_dve)

        def _aexp(e):
            e.activation(out=Atile["f"][:], in_=Atile["f"][:], func=ACTF.Exp)
            return e.activation(out=Atile["b"][:], in_=Atile["b"][:], func=ACTF.Exp)
        t_aexp = stage("scalar", _aexp, waits=[(s_ld, tcst)], inc=s_act)

        def _aneg(e):
            e.tensor_scalar(out=Atile["f"][:], in0=Atile["f"][:], scalar1=-1.0, scalar2=None, op0=OP.mult)
            return e.tensor_scalar(out=Atile["b"][:], in0=Atile["b"][:], scalar1=-1.0, scalar2=None, op0=OP.mult)
        t_aneg = stage("vector", _aneg, waits=[(s_act, t_aexp)], inc=s_dve)

        def _init(e):
            for i in range(2):
                e.memset(hsl[i][:], 0.0)
                e.memset(dasl[i][:], 0.0)
                e.memset(usl[i][:], 0.0)
            e.memset(carry["f"][:], 0.0)
            e.memset(carry["b"][:], 0.0)
            e.memset(halo[:], 0.0)
            return e.memset(xsl[:], 0.0)
        t_init = stage("vector", _init, inc=s_dve)

        stA = contextlib.ExitStack()
        WinT16 = stA.enter_context(nc.sbuf_tensor("WinT16", [128, MG * 2 * DI], BF16))

        t_wincast = 0
        QW = 1024
        with nc.sbuf_tensor("wstage", [128, QW], F32) as wstage:
            for mi in range(MG):
                for qw in range(2 * DI // QW):
                    t1 = stage("sync",
                               lambda e, mi=mi, qw=qw: e.dma_start(
                                   out=wstage[:], in_=d_WinT[mi * 128:(mi + 1) * 128, qw * QW:(qw + 1) * QW]),
                               waits=[(s_dve, t_wincast)], inc=s_ld, inck=16)
                    t_wincast = stage(
                        "vector",
                        lambda e, mi=mi, qw=qw: e.tensor_copy(
                            out=WinT16[:, mi * 2 * DI + qw * QW:mi * 2 * DI + (qw + 1) * QW], in_=wstage[:]),
                        waits=[(s_ld, t1)], inc=s_dve)

        if DEBUG_STASH:
            stage("gpsimd", lambda e: e.dma_start(out=d_dbgw, in_=WinT16[:]),
                  waits=[(s_dve, t_wincast)], inc=s_st, inck=16)
        u32 = [stA.enter_context(nc.sbuf_tensor("u32_0", [128, DM], F32))] * 2
        uT16 = [stA.enter_context(nc.sbuf_tensor(f"uT16_{i}", [128, MG * TC], BF16)) for i in range(2)]

        state = {
            "margin": 0,
            "xproj_prev": 0,
            "dtmm_prev": 0,
            "scan_prev": 0,
            "bcw_prev": 0,
            "bc_read": 0,
            "slab_free": [0, 0],
            "da_prev": [0, 0],
            "ypsum_free": [0, 0],
            "ysl_free": [0, 0],
            "eplg": [0] * NG,
            "xc": [0] * NG,
            "dtg": [0] * NG,
            "uT_free": [0, 0],
            "u32_free": [t_wincast, t_wincast],
            "u32_pe": [0, 0],
            "tpb_free": [0, 0],
            "outp_prev": 0,
            "yfw_ready": 0,
            "ybw": [0] * NG,
        }

        def emit_mixer_chunk(br, ph, kk, tc):
            DOF = 3

            for g in range(NG):
                xw = xsl[:, g * CW:(g + 1) * CW]

                def _conv(e, g=g, xw=xw):
                    if ph == 0:
                        e.tensor_copy(out=xw[:, 0:3], in_=halo[:, g * 3:(g + 1) * 3])
                        e.tensor_copy(out=halo[:, g * 3:(g + 1) * 3], in_=xw[:, TC:TC + 3])
                    else:
                        e.tensor_copy(out=xw[:, TC + 3:TC + 6], in_=halo[:, g * 3:(g + 1) * 3])
                        e.tensor_copy(out=halo[:, g * 3:(g + 1) * 3], in_=xw[:, 3:6])
                    offs = [(k, (k if ph == 0 else 6 - k)) for k in (3, 2, 1, 0)]
                    k0, o0 = offs[0]
                    e.tensor_scalar(out=conv_acc[:], in0=xw[:, o0:o0 + TC],
                                    scalar1=convw[br][:, g * K + k0:g * K + k0 + 1],
                                    scalar2=None, op0=OP.mult)
                    for k, o in offs[1:]:
                        last = e.scalar_tensor_tensor(out=conv_acc[:], in0=xw[:, o:o + TC],
                                                      scalar=convw[br][:, g * K + k:g * K + k + 1],
                                                      in1=conv_acc[:], op0=OP.mult, op1=OP.add)
                    return last
                prev_silu = state["xc"][g - 1] if g > 0 else state["xc"][NG - 1]
                t_conv = stage("vector", _conv,
                               waits=[(s_act, state["xin_ready"][g]), (s_act, prev_silu)],
                               inc=s_dve)
                state["xc"][g] = stage(
                    "scalar",
                    lambda e, g=g: e.activation(out=xsl[:, g * CW + DOF:g * CW + DOF + TC],
                                                in_=conv_acc[:], func=ACTF.Silu,
                                                bias=convb[br][:, g:g + 1]),
                    waits=[(s_dve, t_conv), (s_st, state["xstash"][g])], inc=s_act)

            for g in range(NG):
                waits = [(s_act, state["xc"][g])]
                if g == 0:
                    waits.append((s_act, state["bc_evac"]))
                    waits.append((s_dve, state["pb2_free"]))
                t_xp = stage(
                    "tensor",
                    lambda e, g=g: e.matmul(out=pb[2][0:96, 0:TC],
                                            lhsT=xpT[br][:, g * 96:(g + 1) * 96],
                                            rhs=xsl[:, g * CW + DOF:g * CW + DOF + TC],
                                            start=(g == 0), stop=(g == NG - 1)),
                    waits=waits, inc=(s_pe if g == NG - 1 else None))
            state["xproj_prev"] = t_xp

            def _bcevac(e):
                e.activation(out=dtpre[:], in_=pb[2][0:RNK, 0:TC], func=ACTF.Copy)
                return e.activation(out=bcrow[RNK:96, :], in_=pb[2][RNK:96, 0:TC], func=ACTF.Copy)
            state["bc_evac"] = stage("scalar", _bcevac,
                                     waits=[(s_pe, t_xp), (s_st, state["bcw_prev"]),
                                            (s_pe, state["dtmm_prev"])],
                                     inc=s_act)
            t_dtpre = state["bc_evac"]

            state["bcw_prev"] = stage(
                "gpsimd",
                lambda e, tc=tc, ph=ph: e.dma_start(out=s_bc[:, (ph * NTC + tc) % (L // TC) * TC:
                                                             ((ph * NTC + tc) % (L // TC) + 1) * TC],
                                                    in_=bcrow[RNK:96, :]),
                waits=[(s_act, state["bc_evac"])], inc=s_st, inck=16)

            def _bcread(e, tc=tc, ph=ph):
                col = (ph * NTC + tc) % (L // TC)
                src = s_bc[:, col * TC:(col + 1) * TC]
                bsrc = src[0:NST, :].unsqueeze(0).to_broadcast([128, NST, TC])
                csrc = src[NST:32, :].unsqueeze(0).to_broadcast([128, NST, TC])
                e.dma_start(out=Bsl[:].rearrange("p (n t) -> p n t", n=NST), in_=bsrc)
                return e.dma_start(out=Csl[:].rearrange("p (n t) -> p n t", n=NST), in_=csrc)
            w("sync", s_st, state["bcw_prev"])
            w("sync", s_dve, state["scan_prev"])
            w("sync", s_gp, s_gp.n)
            _v = s_ld.take(32)
            def _bcread2(e, tc=tc, ph=ph):
                col = (ph * NTC + tc) % (L // TC)
                src = s_bc[:, col * TC:(col + 1) * TC]
                bsrc = src[0:NST, :].unsqueeze(0).to_broadcast([128, NST, TC])
                csrc = src[NST:32, :].unsqueeze(0).to_broadcast([128, NST, TC])
                e.dma_start(out=Bsl[:].rearrange("p (n t) -> p n t", n=NST), in_=bsrc).then_inc(s_ld.h, 16)
                e.dma_start(out=Csl[:].rearrange("p (n t) -> p n t", n=NST), in_=csrc).then_inc(s_ld.h, 16)
            prog["sync"].append(_bcread2)
            state["bc_read"] = _v

            for g in range(NG):
                mmw = [(s_act, t_dtpre if g == 0 else state["dt_evac"]),
                       (s_act, state["dtg"][g])]
                if g == 0:
                    mmw.append((s_dve, state["pb3_free"]))
                t_mm = stage(
                    "tensor",
                    lambda e, g=g: e.matmul(out=pb[3][:, 0:TC],
                                            lhsT=dtwT[br][:, g * 128:(g + 1) * 128],
                                            rhs=dtpre[:], start=True, stop=True),
                    waits=mmw, inc=s_pe)

                def _dt(e, g=g):
                    e.activation(out=etmp[:], in_=pb[3][:, 0:TC], func=ACTF.Exp,
                                 bias=dtb[br][:, g:g + 1])
                    return e.activation(out=dtsl[:, g * TC:(g + 1) * TC], in_=etmp[:],
                                        func=ACTF.Ln, bias=1.0)
                state["dtg"][g] = stage("scalar", _dt,
                                        waits=[(s_pe, t_mm), (s_dve, state["dt_used"][g]),
                                               (s_st, state.get("dbg_tok", 0))],
                                        inc=s_act)
                state["dt_evac"] = state["dtg"][g]
            state["dtmm_prev"] = t_mm

            units = [(g, q) for g in range(NG) for q in range(NU)]
            NUNITS = len(units)
            rot_of = lambda ui: ui % 2

            def emit_w(g):
                stage("vector",
                      lambda e, g=g: e.tensor_tensor(
                          out=wt[:], in0=dtsl[:, g * TC:(g + 1) * TC],
                          in1=xsl[:, g * CW + DOF:g * CW + DOF + TC],
                          op=OP.mult),
                      waits=[(s_act, state["dtg"][g]), (s_gp, s_gp.n)], inc=s_dve)
                state["wt_tok"] = s_dve.n

            def emit_arg(ui):
                g, q = units[ui]
                rot = rot_of(ui)
                def _arg(e, g=g, q=q, rot=rot):
                    dv = dasl[rot][:].rearrange("p (n t) -> p n t", n=NQ)
                    e.memzero(dv[:, :, 0:1])
                    e.memzero(dv[:, :, TP2 - 1:TP2])
                    for j in range(NQ):
                        n = q * NQ + j
                        last = e.activation(
                            out=dasl[rot][:, j * TP2 + 1:j * TP2 + 1 + TC],
                            in_=dtsl[:, g * TC:(g + 1) * TC], func=ACTF.Identity,
                            scale=Atile[br][:, g * NST + n:g * NST + n + 1])
                    return last
                waits = [(s_dve, state["scan_tok"][rot])]
                state["arg_tok"][rot] = stage("scalar", _arg, waits=waits, inc=s_act)

            def emit_exp(ui):
                rot = rot_of(ui)
                state["exp_tok"][rot] = stage(
                    "scalar",
                    lambda e, rot=rot: e.activation(out=dasl[rot][:], in_=dasl[rot][:],
                                                    func=ACTF.Exp),
                    waits=[], inc=s_act)

            def emit_u(ui):
                g, q = units[ui]
                rot = rot_of(ui)
                def _u(e, g=g, q=q, rot=rot, ph=ph):
                    pfx = 0 if ph == 0 else TP2 - 1
                    e.tensor_copy(
                        out=usl[rot][:].rearrange("p (n t) -> p n t", n=NQ)[:, :, pfx:pfx + 1],
                        in_=carry[br][:, g * NST + q * NQ:g * NST + (q + 1) * NQ]
                            .unsqueeze(2))
                    wv = wt[:].unsqueeze(1).to_broadcast([128, NQ, TC])
                    return e.tensor_tensor(
                        out=usl[rot][:].rearrange("p (n t) -> p n t", n=NQ)[:, :, 1:1 + TC],
                        in0=wv,
                        in1=Bsl[:].rearrange("p (n t) -> p n t", n=NST)[:, q * NQ:(q + 1) * NQ, :],
                        op=OP.mult)
                waits = [(s_dve, state["usl_free"][rot]), (s_ld, state["bc_read"]),
                         (s_dve, state["wt_tok"]),
                         (s_dve, state["carry_tok"][(br, g, q)])]
                state["u_tok"][rot] = stage("gpsimd", _u, waits=waits, inc=s_gp)

            def emit_carrysave(ui):
                def _cs(e, ui=ui, ph=ph):
                    pg, pq = units[ui]
                    cs = TC if ph == 0 else 1
                    return e.tensor_copy(
                        out=carry[br][:, pg * NST + pq * NQ:pg * NST + (pq + 1) * NQ]
                            .unsqueeze(2),
                        in_=hsl[rot_of(ui)][:].rearrange("p (n t) -> p n t", n=NQ)[:, :, cs:cs + 1])
                pg, pq = units[ui]
                state["carry_tok"][(br, pg, pq)] = stage("vector", _cs, inc=s_dve)

            def emit_scan(ui):
                g, q = units[ui]
                rot = rot_of(ui)
                def _scan(e, g=g, q=q, rot=rot, ph=ph):
                    for j in range(NQ):
                        if ph == 0:
                            last = e.tensor_tensor_scan(
                                out=hsl[rot][:, j * TP2:j * TP2 + 1 + TC],
                                data0=dasl[rot][:, j * TP2:j * TP2 + 1 + TC],
                                data1=usl[rot][:, j * TP2:j * TP2 + 1 + TC],
                                initial=0.0, op0=OP.mult, op1=OP.add)
                        else:
                            last = e.tensor_tensor_scan(
                                out=hsl[rot][:, j * TP2 + 1:(j + 1) * TP2][:, ::-1],
                                data0=dasl[rot][:, j * TP2 + 1:(j + 1) * TP2][:, ::-1],
                                data1=usl[rot][:, j * TP2 + 1:(j + 1) * TP2][:, ::-1],
                                initial=0.0, op0=OP.mult, op1=OP.add)
                    return last
                waits = [(s_act, state["exp_tok"][rot]), (s_gp, state["u_tok"][rot]),
                         (s_gp, state["cttg_tok"][rot]), (s_pe, state["slab_free"][rot])]
                state["scan_tok"][rot] = stage("vector", _scan, waits=waits, inc=s_dve)
                state["usl_free"][rot] = state["scan_tok"][rot]

            def emit_ctt(ui):
                g, q = units[ui]
                rot = rot_of(ui)
                NH = NQ // 2
                def _cg(e, g=g, q=q, rot=rot):
                    return e.tensor_tensor(
                        out=htsl[rot][:].rearrange("p (n t) -> p n t", n=NQ)[:, 0:NH, :],
                        in0=hsl[rot][:].rearrange("p (n t) -> p n t", n=NQ)[:, 0:NH, 1:1 + TC],
                        in1=Csl[:].rearrange("p (n t) -> p n t", n=NST)[:, q * NQ:q * NQ + NH, :],
                        op=OP.mult)
                state["cttg_tok"][rot] = stage(
                    "gpsimd", _cg,
                    waits=[(s_dve, state["scan_tok"][rot]), (s_pe, state["slab_free"][rot])],
                    inc=s_gp)
                def _cv(e, g=g, q=q, rot=rot):
                    return e.tensor_tensor(
                        out=htsl[rot][:].rearrange("p (n t) -> p n t", n=NQ)[:, NH:NQ, :],
                        in0=hsl[rot][:].rearrange("p (n t) -> p n t", n=NQ)[:, NH:NQ, 1:1 + TC],
                        in1=Csl[:].rearrange("p (n t) -> p n t", n=NST)[:, q * NQ + NH:(q + 1) * NQ, :],
                        op=OP.mult)
                waits = [(s_pe, state["slab_free"][rot])]
                state["ctt_tok"][rot] = stage("vector", _cv, waits=waits, inc=s_dve)

            def emit_acc(ui):
                g, q = units[ui]
                rot = rot_of(ui)
                ybank = 4 + (g % 2)
                waits = [(s_dve, state["ctt_tok"][rot]), (s_gp, state["cttg_tok"][rot])]
                if q == 0:
                    waits += [(s_dve, state["ypsum_free"][g % 2]), (s_dve, t_ident)]
                def _acc(e, g=g, q=q, rot=rot, ybank=ybank):
                    for j in range(NQ):
                        last = e.matmul(out=pb[ybank][:, 0:TC], lhsT=ident[:],
                                        rhs=htsl[rot][:, j * TC:(j + 1) * TC],
                                        start=(q == 0 and j == 0),
                                        stop=(q == NU - 1 and j == NQ - 1))
                    return last
                state["slab_free"][rot] = stage("tensor", _acc, waits=waits, inc=s_pe)
                return state["slab_free"][rot]

            def emit_epilogue(g, t_yacc):
                yrot = g % 2
                if ph == 0:
                    ytarget = ysl[yrot][:]
                    ywaits = [(s_st, state["ysl_free"][yrot])]
                else:
                    ytarget = ybw[:, g * TC:(g + 1) * TC]
                    ywaits = [(s_pe, state["outp_prev"]), (s_st, state.get("dbg_tok", 0))]

                def _eplg(e, g=g, yrot=yrot, ytarget=ytarget):
                    e.scalar_tensor_tensor(
                        out=conv_acc[:],
                        in0=xsl[:, g * CW + DOF:g * CW + DOF + TC],
                        scalar=Dv[br][:, g:g + 1],
                        in1=pb[4 + yrot][:, 0:TC], op0=OP.mult, op1=OP.add)
                    return e.tensor_tensor(out=ytarget, in0=conv_acc[:],
                                           in1=szsl[:, g * TC:(g + 1) * TC], op=OP.mult)
                t_ep = stage("vector", _eplg,
                             waits=[(s_pe, t_yacc), (s_act, state["sz_ready"][g]),
                                    (s_act, state["xc"][NG - 1])] + ywaits,
                             inc=s_dve)
                state["eplg"][g] = t_ep
                state["ypsum_free"][yrot] = t_ep
                state["dt_used"][g] = t_ep
                if ph == 0:
                    state["ysl_free"][yrot] = stage(
                        "gpsimd",
                        lambda e, g=g, kk=kk, yrot=yrot: e.dma_start(
                            out=s_yfwd[g * 128:(g + 1) * 128, kk * TC:(kk + 1) * TC],
                            in_=ysl[yrot][:]),
                        waits=[(s_dve, t_ep)], inc=s_st, inck=16)
                else:
                    state["ybw"][g] = t_ep

            emit_w(0)
            emit_arg(0)
            emit_exp(0)
            emit_u(0)
            for ui in range(NUNITS):
                g, q = units[ui]
                if ui >= 2:
                    emit_carrysave(ui - 2)
                if ui + 1 < NUNITS:
                    gn, qn = units[ui + 1]
                    if qn == 0:
                        emit_w(gn)
                    emit_arg(ui + 1)
                    emit_u(ui + 1)
                emit_scan(ui)
                if ui + 1 < NUNITS:
                    emit_exp(ui + 1)
                emit_ctt(ui)
                t_yacc = emit_acc(ui)
                if q == NU - 1:
                    emit_epilogue(g, t_yacc)
            emit_carrysave(NUNITS - 2)
            stage("vector",
                  lambda e: e.tensor_copy(out=conv_acc[:],
                                          in_=hsl[rot_of(NUNITS - 1)][:, 0:TC]),
                  inc=s_dve)
            emit_carrysave(NUNITS - 1)
            state["scan_prev"] = state["scan_tok"][rot_of(NUNITS - 1)]

        state["bc_evac"] = 0
        state["carry_tok"] = {("f", g, q): 0 for g in range(NG) for q in range(NU)}
        state["carry_tok"].update({("b", g, q): 0 for g in range(NG) for q in range(NU)})
        state["wt_tok"] = 0
        state["wt_gp"] = 0
        state["arg_tok"] = [0, 0]
        state["exp_tok"] = [0, 0]
        state["u_tok"] = [0, 0]
        state["scan_tok"] = [0, 0]
        state["ctt_tok"] = [0, 0]
        state["cttg_tok"] = [0, 0]
        state["usl_free"] = [0, 0]
        state["dt_evac"] = 0
        state["dt_used"] = [0] * NG
        state["xin_ready"] = [0] * NG
        state["sz_ready"] = [0] * NG
        state["xstash"] = [0] * NG
        state["pb2_free"] = 0
        state["dbg_tok"] = 0
        state["pb3_free"] = t_round
        state["szstash_prev"] = 0

        for tc in range(NTC):
            kk = tc
            rot = tc % 2

            if tc == 0:
                w("scalar", s_dve, t_init)

            for ts in range(TSUB):
                i = (tc * TSUB + ts) % 2
                t_load = stage("sync",
                               lambda e, i=i, kk=kk, ts=ts: e.dma_start(
                                   out=u32[i][:], in_=d_u[kk * TC + ts * 128:kk * TC + (ts + 1) * 128, :]),
                               waits=[(s_dve, state["u32_free"][i]), (s_pe, state["u32_pe"][i])],
                               inc=s_ld, inck=16)
                state["u32_free"][i] = 0
                for half in range(2):
                    def _tp(e, i=i, half=half):
                        for mm in range(4):
                            m = half * 4 + mm
                            last = e.matmul(out=pb[6 + half][:, mm * 128:(mm + 1) * 128],
                                            lhsT=u32[i][:, m * 128:(m + 1) * 128],
                                            rhs=ident32[:], is_transpose=True,
                                            start=(mm == 0), stop=(mm == 3))
                        return last
                    t_tp = stage("tensor", _tp,
                                 waits=[(s_ld, t_load), (s_act, state["tpb_free"][half])],
                                 inc=s_pe)
                    state["u32_pe"][0] = t_tp
                    state["u32_pe"][1] = t_tp
                    def _ev(e, rot=rot, ts=ts, half=half):
                        for mm in range(4):
                            m = half * 4 + mm
                            last = e.activation(
                                out=uT16[rot][:, m * TC + ts * 128:m * TC + ts * 128 + 128],
                                in_=pb[6 + half][:, mm * 128:(mm + 1) * 128], func=ACTF.Copy)
                        return last
                    state["tpb_free"][half] = stage("scalar", _ev, waits=[(s_pe, t_tp)], inc=s_act)
            t_uT = state["tpb_free"][1]

            t_evac_po = [0] * 32
            for po in range(32):
                bank = po % 2
                waits = [(s_act, t_uT)]
                if po >= 2:
                    waits.append((s_act, t_evac_po[po - 2]))
                if po == 0:
                    waits.append((s_dve, t_wincast))
                def _mm(e, po=po, bank=bank, rot=rot):
                    for mi in range(MG):
                        last = e.matmul(out=pb[bank][:, 0:TC],
                                        lhsT=WinT16[:, mi * 2 * DI + po * 128:mi * 2 * DI + (po + 1) * 128],
                                        rhs=uT16[rot][:, mi * TC:(mi + 1) * TC],
                                        start=(mi == 0), stop=(mi == MG - 1))
                    return last
                t_mm = stage("tensor", _mm, waits=waits, inc=s_pe)
                if po < NG:
                    g = po
                    state["xin_ready"][g] = stage(
                        "scalar",
                        lambda e, g=g, bank=bank: e.activation(
                            out=xsl[:, g * CW + 3:g * CW + 3 + TC],
                            in_=pb[bank][:, 0:TC], func=ACTF.Copy),
                        waits=[(s_pe, t_mm), (s_dve, state["eplg"][g]),
                               (s_pe, state["xproj_prev"]),
                               (s_st, state.get("dbg_tok", 0))],
                        inc=s_act)
                    t_evac_po[po] = state["xin_ready"][g]
                else:
                    g = po - NG
                    state["sz_ready"][g] = stage(
                        "scalar",
                        lambda e, g=g, bank=bank: e.activation(
                            out=szsl[:, g * TC:(g + 1) * TC],
                            in_=pb[bank][:, 0:TC], func=ACTF.Silu),
                        waits=[(s_pe, t_mm), (s_dve, state["eplg"][g]),
                               (s_st, state["szstash_prev"])],
                        inc=s_act)
                    t_evac_po[po] = state["sz_ready"][g]
            state["uT_free"][rot] = t_mm

            for g in range(NG):
                state["xstash"][g] = stage(
                    "gpsimd",
                    lambda e, g=g, kk=kk: e.dma_start(
                        out=s_xin[g * 128:(g + 1) * 128, kk * TC:(kk + 1) * TC],
                        in_=xsl[:, g * CW + 3:g * CW + 3 + TC]),
                    waits=[(s_act, t_evac_po[g])], inc=s_st, inck=16)
            state["szstash_prev"] = stage(
                "gpsimd",
                lambda e, kk=kk: e.dma_start(
                    out=s_siluz.rearrange("(g p) l -> p g l", p=128)[:, :, kk * TC:(kk + 1) * TC],
                    in_=szsl[:].rearrange("p (g c) -> p g c", g=NG)),
                waits=[(s_act, t_evac_po[31])], inc=s_st, inck=16)

            emit_mixer_chunk("f", 0, kk, tc)
            if DEBUG_STASH:
                stage("gpsimd",
                      lambda e, kk=kk: e.dma_start(
                          out=d_dbgxc.rearrange("(g p) l -> p g l", p=128)[:, :, kk * TC:(kk + 1) * TC],
                          in_=xsl[:].rearrange("p (g c) -> p g c", g=NG)[:, :, 3:3 + TC]),
                      waits=[(s_dve, state["eplg"][NG - 1])], inc=s_st, inck=16)
                state["dbg_tok"] = stage("gpsimd",
                      lambda e, kk=kk: e.dma_start(
                          out=d_dbgdt.rearrange("(g p) l -> p g l", p=128)[:, :, kk * TC:(kk + 1) * TC],
                          in_=dtsl[:].rearrange("p (g c) -> p g c", g=NG)),
                      waits=[(s_dve, state["eplg"][NG - 1])], inc=s_st, inck=16)

        stA.close()
        w("sync", s_ld, s_ld.n)
        w("gpsimd", s_st, s_st.n)
        for eng in ["sync", "gpsimd", "vector", "scalar", "tensor"]:
            s_ph.take(1)
            def _barrier(e):
                e.nop().then_inc(s_ph.h, 1)
            prog[eng].append(_barrier)
        for eng in ["sync", "gpsimd", "vector", "scalar", "tensor"]:
            w(eng, s_ph, 5)

        stB = contextlib.ExitStack()
        ybw = stB.enter_context(nc.sbuf_tensor("ybw", [128, NG * TC], BF16))
        yfw = stB.enter_context(nc.sbuf_tensor("yfw", [128, NG * TC], BF16))
        WoutT16 = stB.enter_context(nc.sbuf_tensor("WoutT16", [128, NG * DM], BF16))
        WfT16 = stB.enter_context(nc.sbuf_tensor("WfT16", [128, MG * DM], BF16))
        ures = stB.enter_context(nc.sbuf_tensor("ures", [128, DM], F32))
        x1 = stB.enter_context(nc.sbuf_tensor("x1", [128, DM], F32))
        x1T = stB.enter_context(nc.sbuf_tensor("x1T", [128, MG * 128], BF16))
        outsl = stB.enter_context(nc.sbuf_tensor("outsl", [128, DM], F32))
        lncol = lncol0
        wstageB = ures

        t_wc = 0
        for g in range(NG):
            t1 = stage("sync", lambda e, g=g: e.dma_start(out=wstageB[:], in_=d_WoutT[g * 128:(g + 1) * 128, :]),
                       waits=[(s_dve, t_wc)], inc=s_ld, inck=16)
            t_wc = stage("vector",
                         lambda e, g=g: e.tensor_copy(out=WoutT16[:, g * DM:(g + 1) * DM], in_=wstageB[:]),
                         waits=[(s_ld, t1)], inc=s_dve)
        t_ws = 0
        for m in range(MG):
            t1 = stage("sync", lambda e, m=m: e.dma_start(out=wstageB[:], in_=d_WfT[m * 128:(m + 1) * 128, :]),
                       waits=[(s_act, t_ws), (s_dve, t_wc if m == 0 else 0)], inc=s_ld, inck=16)
            t_ws = stage("scalar",
                         lambda e, m=m: e.activation(out=WfT16[:, m * DM:(m + 1) * DM], in_=wstageB[:],
                                                     func=ACTF.Sign),
                         waits=[(s_ld, t1)], inc=s_act)
        t_wB = t_ws

        pm = {
            "ures_free": 0, "x1T_free": [0, 0], "attn_free": [0] * 4,
            "ffn_free": [0, 0], "out_free": 0, "tp_free": [0, 0],
            "yfw_free": 0, "lncol_free": 0,
        }

        def emit_ln(buf, tok_data):
            def _a1(e):
                e.activation(out=ures[:], in_=buf[:], func=ACTF.Copy, accum_out=lncol[:, 0:1])
                e.activation(out=ures[:], in_=buf[:], func=ACTF.Square, accum_out=lncol[:, 2:3])
                e.activation(out=lncol[:, 1:2], in_=lncol[:, 0:1], func=ACTF.Copy, scale=1.0 / DM)
                return e.activation(out=lncol[:, 12:13], in_=lncol[:, 0:1], func=ACTF.Square,
                                    scale=1.0 / DM)
            t_a1 = stage("scalar", _a1, waits=[(s_dve, tok_data)], inc=s_act)
            t_v1 = stage("vector",
                         lambda e: e.scalar_tensor_tensor(out=lncol[:, 13:14], in0=epsb[:],
                                                          scalar=lncol[:, 12:13], in1=epsb[:],
                                                          op0=OP.subtract, op1=OP.bypass),
                         waits=[(s_act, t_a1)], inc=s_dve)
            t_a2 = stage("scalar",
                         lambda e: e.activation(out=lncol[:, 3:4], in_=lncol[:, 2:3],
                                                func=ACTF.Sqrt, scale=1.0 / DM,
                                                bias=lncol[:, 13:14]),
                         waits=[(s_dve, t_v1)], inc=s_act)
            t_v2 = stage("vector",
                         lambda e: e.reciprocal(out=lncol[:, 16:24], in_=lncol[:, 3:11]),
                         waits=[(s_act, t_a2)], inc=s_dve)
            t_a3 = stage("scalar",
                         lambda e: e.activation(out=lncol[:, 15:16], in_=lncol[:, 16:17],
                                                func=ACTF.Copy),
                         waits=[(s_dve, t_v2)], inc=s_act)
            t_ap = stage("vector",
                         lambda e: e.scalar_tensor_tensor(
                             out=buf[:], in0=buf[:], scalar=lncol[:, 1:2],
                             in1=lncol[:, 16:17].to_broadcast([128, DM]),
                             op0=OP.subtract, op1=OP.mult),
                         waits=[(s_act, t_a3)], inc=s_dve)
            return t_ap

        def emit_postmixer(kk):
            t_yf = stage("sync",
                         lambda e, kk=kk: e.dma_start(
                             out=yfw[:].rearrange("p (g c) -> p g c", g=NG),
                             in_=s_yfwd.rearrange("(g p) l -> p g l", p=128)[:, :, kk * TC:(kk + 1) * TC]),
                         waits=[(s_pe, pm["yfw_free"])], inc=s_ld, inck=16)
            for ts in range(TSUB):
                t_u = stage("sync",
                            lambda e, kk=kk, ts=ts: e.dma_start(
                                out=ures[:], in_=d_u[kk * TC + ts * 128:kk * TC + (ts + 1) * 128, :]),
                            waits=[(s_dve, pm["ures_free"]), (s_act, t_wB)], inc=s_ld, inck=16)
                for fo in range(2):
                    bank = fo
                    waits = [(s_ld, t_yf), (s_dve, pm["attn_free"][bank]), (s_act, t_wB)]
                    waits += [(s_dve, state["ybw"][g]) for g in range(NG)] if (ts == 0 and fo == 0) else []
                    def _op(e, ts=ts, fo=fo, bank=bank):
                        for g in range(NG):
                            e.matmul(out=pb[bank][:, 0:512],
                                     lhsT=yfw[:, g * TC + ts * 128:g * TC + ts * 128 + 128],
                                     rhs=WoutT16[:, g * DM + fo * 512:g * DM + fo * 512 + 512],
                                     start=(g == 0), stop=False)
                        for g in range(NG):
                            last = e.matmul(out=pb[bank][:, 0:512],
                                            lhsT=ybw[:, g * TC + ts * 128:g * TC + ts * 128 + 128],
                                            rhs=WoutT16[:, g * DM + fo * 512:g * DM + fo * 512 + 512],
                                            start=False, stop=(g == NG - 1))
                        return last
                    t_at = stage("tensor", _op, waits=waits, inc=s_pe)
                    pm["attn_bank_tok"][bank] = t_at
                pm["outp_tok"] = t_at

                if DEBUG_STASH:
                    def _dbga(e):
                        e.tensor_copy(out=x1t16[:, 0:512], in_=pb[0][:, 0:512])
                        return e.tensor_copy(out=x1t16[:, 512:1024], in_=pb[1][:, 0:512])
                    t_dbga = stage("vector", _dbga,
                          waits=[(s_pe, pm["attn_bank_tok"][0]), (s_pe, pm["attn_bank_tok"][1])],
                          inc=s_dve)
                    pm["dbga_tok"] = stage("gpsimd",
                          lambda e, kk=kk, ts=ts: e.dma_start(
                              out=d_dbgat[kk * TC + ts * 128:kk * TC + (ts + 1) * 128, :],
                              in_=x1t16[:]),
                          waits=[(s_dve, t_dbga)], inc=s_st, inck=16)
                def _res(e, ts=ts):
                    e.tensor_tensor(out=x1[:, 0:512], in0=ures[:, 0:512], in1=pb[0][:, 0:512], op=OP.add)
                    return e.tensor_tensor(out=x1[:, 512:1024], in0=ures[:, 512:1024],
                                           in1=pb[1][:, 0:512], op=OP.add)
                t_x1 = stage("vector", _res,
                             waits=[(s_ld, t_u), (s_pe, pm["attn_bank_tok"][1]),
                                    (s_dve, pm["lncol_free"]), (s_st, pm.get("dbga_tok", 0))],
                             inc=s_dve)

                pm["attn_free"][0] = t_x1
                pm["attn_free"][1] = t_x1

                t_x1n = emit_ln(x1, t_x1)

                if DEBUG_STASH:
                    stage("gpsimd",
                          lambda e, kk=kk, ts=ts: e.dma_start(
                              out=d_dbgx1[kk * TC + ts * 128:kk * TC + (ts + 1) * 128, :],
                              in_=x1[:]),
                          waits=[(s_dve, t_x1n)], inc=s_st, inck=16)
                for half in range(2):
                    def _tp(e, half=half):
                        for mm in range(4):
                            m = half * 4 + mm
                            last = e.matmul(out=pb[6 + half][:, mm * 128:(mm + 1) * 128],
                                            lhsT=x1[:, m * 128:(m + 1) * 128],
                                            rhs=ident32[:], is_transpose=True,
                                            start=(mm == 0), stop=(mm == 3))
                        return last
                    t_tp = stage("tensor", _tp,
                                 waits=[(s_dve, t_x1n), (s_act, pm["tp_free"][half])], inc=s_pe)
                    def _ev(e, half=half):
                        for mm in range(4):
                            m = half * 4 + mm
                            last = e.activation(out=x1T[:, m * 128:(m + 1) * 128],
                                                in_=pb[6 + half][:, mm * 128:(mm + 1) * 128],
                                                func=ACTF.Copy)
                        return last
                    pm["tp_free"][half] = stage("scalar", _ev,
                                                waits=[(s_pe, t_tp), (s_pe, pm["x1T_free"][half])],
                                                inc=s_act)
                t_x1T = pm["tp_free"][1]

                for fo in range(2):
                    def _ffn(e, fo=fo):
                        for m in range(MG):
                            last = e.matmul(out=pb[2 + fo][:, 0:512],
                                            lhsT=x1T[:, m * 128:(m + 1) * 128],
                                            rhs=WfT16[:, m * DM + fo * 512:m * DM + fo * 512 + 512],
                                            start=(m == 0), stop=(m == MG - 1))
                        return last
                    t_ffn = stage("tensor", _ffn,
                                  waits=[(s_act, t_x1T), (s_dve, pm["ffn_free"][fo])], inc=s_pe)
                pm["x1T_free"][0] = t_ffn
                pm["x1T_free"][1] = t_ffn

                def _res2(e):
                    e.tensor_tensor(out=outsl[:, 0:512], in0=x1[:, 0:512], in1=pb[2][:, 0:512], op=OP.add)
                    return e.tensor_tensor(out=outsl[:, 512:1024], in0=x1[:, 512:1024],
                                           in1=pb[3][:, 0:512], op=OP.add)
                t_o1 = stage("vector", _res2, waits=[(s_pe, t_ffn), (s_out, pm["out_free"])], inc=s_dve)
                pm["ffn_free"][0] = t_o1
                pm["ffn_free"][1] = t_o1
                pm["pb2_free"] = t_o1
                pm["pb3_free"] = t_o1

                t_o2 = emit_ln(outsl, t_o1)
                pm["lncol_free"] = t_o2

                pm["out_free"] = stage(
                    "gpsimd",
                    lambda e, kk=kk, ts=ts: e.dma_start(
                        out=d_out[kk * TC + ts * 128:kk * TC + (ts + 1) * 128, :], in_=outsl[:]),
                    waits=[(s_dve, t_o2)], inc=s_out, inck=16)
                pm["ures_free"] = t_o2
            pm["yfw_free"] = pm["outp_tok"]

        pm["attn_bank_tok"] = [0] * 4
        pm["outp_tok"] = 0
        pm["pb2_free"] = 0
        pm["pb3_free"] = 0

        t_hz = stage("vector", lambda e: e.memset(halo[:], 0.0), inc=s_dve)
        for g in range(NG):
            state["xstash"][g] = 0
        state["szstash_prev"] = 0

        for tcB in range(NTC):
            kk = NTC - 1 - tcB

            t_x = stage("sync",
                        lambda e, kk=kk: e.dma_start(
                            out=xsl[:].rearrange("p (g c) -> p g c", g=NG)[:, :, 3:3 + TC],
                            in_=s_xin.rearrange("(g p) l -> p g l", p=128)[:, :, kk * TC:(kk + 1) * TC]),
                        waits=[(s_dve, max(state["eplg"])), (s_pe, state["xproj_prev"])],
                        inc=s_ld, inck=16)
            t_z = stage("sync",
                        lambda e, kk=kk: e.dma_start(
                            out=szsl[:].rearrange("p (g c) -> p g c", g=NG),
                            in_=s_siluz.rearrange("(g p) l -> p g l", p=128)[:, :, kk * TC:(kk + 1) * TC]),
                        waits=[], inc=s_ld, inck=16)
            for g in range(NG):
                state["xin_ready"][g] = 0
                state["sz_ready"][g] = 0
            w("vector", s_ld, t_x)
            w("vector", s_ld, t_z)
            if tcB == 0:
                w("vector", s_dve, t_hz)

            emit_mixer_chunk("b", 1, kk, tcB)
            if DEBUG_STASH:
                state["dbg_tok"] = stage("gpsimd",
                      lambda e, kk=kk: e.dma_start(
                          out=d_dbgyb.rearrange("(g p) l -> p g l", p=128)[:, :, kk * TC:(kk + 1) * TC],
                          in_=ybw[:].rearrange("p (g c) -> p g c", g=NG)),
                      waits=[(s_dve, state["eplg"][NG - 1])], inc=s_st, inck=16)
            emit_postmixer(kk)
            state["outp_prev"] = pm["outp_tok"]
            state["pb2_free"] = pm["pb2_free"]
            state["pb3_free"] = pm["pb3_free"]

        w("gpsimd", s_out, s_out.n)
        prog["gpsimd"].append(lambda e: e.nop())

        with nc.Block() as block:
            @block.sync
            def _(eng):
                for f in prog["sync"]:
                    f(eng)

            @block.gpsimd
            def _(eng):
                for f in prog["gpsimd"]:
                    f(eng)

            @block.vector
            def _(eng):
                for f in prog["vector"]:
                    f(eng)

            @block.scalar
            def _(eng):
                for f in prog["scalar"]:
                    f(eng)

            @block.tensor
            def _(eng):
                for f in prog["tensor"]:
                    f(eng)

        stB.close()

    return nc


def kernel(**inputs):
    xT = lambda a: np.ascontiguousarray(np.asarray(a, np.float32).T)

    def arr128(v):
        return np.ascontiguousarray(np.asarray(v, np.float32).reshape(NG, 128).T)

    def alog(v):
        return np.ascontiguousarray(
            np.transpose(np.asarray(v, np.float32).reshape(NG, 128, NST), (1, 0, 2)).reshape(128, NG * NST))

    def cw_(v):
        return np.ascontiguousarray(
            np.transpose(np.asarray(v, np.float32).reshape(NG, 128, K), (1, 0, 2)).reshape(128, NG * K))

    def xp_(v):
        vT = np.asarray(v, np.float32).T.reshape(NG, 128, 96)
        return np.ascontiguousarray(np.transpose(vT, (1, 0, 2)).reshape(128, NG * 96))

    base = {
        "WinT": xT(inputs["L0_attn_in_proj_w"]),
        "WoutT": xT(inputs["L0_attn_out_proj_w"]),
        "WfT": xT(inputs["L0_ffn_weight"]),
        "xpT_f": xp_(inputs["L0_attn_fwd_x_proj_w"]),
        "xpT_b": xp_(inputs["L0_attn_bwd_x_proj_w"]),
        "dtwT_f": xT(inputs["L0_attn_fwd_dt_w"]),
        "dtwT_b": xT(inputs["L0_attn_bwd_dt_w"]),
        "Alog_f": alog(inputs["L0_attn_fwd_A_log"]),
        "Alog_b": alog(inputs["L0_attn_bwd_A_log"]),
        "convw_f": cw_(inputs["L0_attn_fwd_conv_w"]),
        "convw_b": cw_(inputs["L0_attn_bwd_conv_w"]),
        "convb_f": arr128(inputs["L0_attn_fwd_conv_b"]),
        "convb_b": arr128(inputs["L0_attn_bwd_conv_b"]),
        "dtb_f": arr128(inputs["L0_attn_fwd_dt_b"]),
        "dtb_b": arr128(inputs["L0_attn_bwd_dt_b"]),
        "Dv_f": arr128(inputs["L0_attn_fwd_D"]),
        "Dv_b": arr128(inputs["L0_attn_bwd_D"]),
    }
    u = np.asarray(inputs["L0_attn_u"], np.float32)
    in_maps = [dict(base, u=np.ascontiguousarray(u[b])) for b in range(B)]

    nc = build_program()
    res = run_bass_kernel_spmd(nc, in_maps, core_ids=list(range(B)))
    return np.stack([np.asarray(res.results[b]["out"], np.float32) for b in range(B)])


# version 09 (final)
# speedup vs baseline: 1.1006x; 1.1006x over previous
import contextlib
import numpy as np

import concourse.bass as bass
import concourse.mybir as mybir
from concourse.bass_utils import run_bass_kernel_spmd
from concourse.alu_op_type import AluOpType

F32 = mybir.dt.float32
F32R = mybir.dt.float32r
BF16 = mybir.dt.bfloat16
ACTF = mybir.ActivationFunctionType
OP = AluOpType
AXX = mybir.AxisListType.X

B = 8
L = 4096
DM = 1024
DI = 2048
NST = 16
RNK = 64
K = 4
NG = DI // 128
MG = DM // 128
TC = 256
NTC = L // TC
TSUB = TC // 128
NQ = 4
NU = NST // NQ
TP2 = TC + 2
CW = TC + 6


class Tok:
    def __init__(self, stack, nc, name):
        self.h = stack.enter_context(nc.semaphore(name))
        self.n = 0

    def take(self, k=1):
        self.n += k
        return self.n


DEBUG_STASH = False

def build_program():
    nc = bass.Bass("TRN2", target_bir_lowering=False, debug=False)

    d_u = nc.dram_tensor("u", [L, DM], F32, kind="ExternalInput").ap()
    d_WinT = nc.dram_tensor("WinT", [DM, 2 * DI], F32, kind="ExternalInput").ap()
    d_WoutT = nc.dram_tensor("WoutT", [DI, DM], F32, kind="ExternalInput").ap()
    d_WfT = nc.dram_tensor("WfT", [DM, DM], F32, kind="ExternalInput").ap()
    d_xpT = {b: nc.dram_tensor(f"xpT_{b}", [128, NG * 96], F32, kind="ExternalInput").ap() for b in "fb"}
    d_dtwT = {b: nc.dram_tensor(f"dtwT_{b}", [RNK, DI], F32, kind="ExternalInput").ap() for b in "fb"}
    d_Alog = {b: nc.dram_tensor(f"Alog_{b}", [128, NG * NST], F32, kind="ExternalInput").ap() for b in "fb"}
    d_convw = {b: nc.dram_tensor(f"convw_{b}", [128, NG * K], F32, kind="ExternalInput").ap() for b in "fb"}
    d_convb = {b: nc.dram_tensor(f"convb_{b}", [128, NG], F32, kind="ExternalInput").ap() for b in "fb"}
    d_dtb = {b: nc.dram_tensor(f"dtb_{b}", [128, NG], F32, kind="ExternalInput").ap() for b in "fb"}
    d_D = {b: nc.dram_tensor(f"Dv_{b}", [128, NG], F32, kind="ExternalInput").ap() for b in "fb"}
    d_out = nc.dram_tensor("out", [L, DM], F32, kind="ExternalOutput").ap()

    _skind = "ExternalOutput" if DEBUG_STASH else "Internal"
    d_dbgw = nc.dram_tensor("dbgw", [128, MG * 2 * DI], BF16, kind="ExternalOutput").ap() if DEBUG_STASH else None
    s_xin = nc.dram_tensor("s_xin", [DI, L], F32, kind=_skind).ap()
    s_siluz = nc.dram_tensor("s_siluz", [DI, L], BF16, kind=_skind).ap()
    s_yfwd = nc.dram_tensor("s_yfwd", [DI, L], BF16, kind=_skind).ap()
    s_bc = nc.dram_tensor("s_bc", [32, L], BF16, kind=_skind).ap()
    d_dbgxc = nc.dram_tensor("dbgxc", [DI, L], F32, kind=_skind).ap() if DEBUG_STASH else None
    d_dbgdt = nc.dram_tensor("dbgdt", [DI, L], F32, kind=_skind).ap() if DEBUG_STASH else None
    d_dbgyb = nc.dram_tensor("dbgyb", [DI, L], BF16, kind=_skind).ap() if DEBUG_STASH else None
    d_dbgx1 = nc.dram_tensor("dbgx1", [L, DM], F32, kind=_skind).ap() if DEBUG_STASH else None
    d_dbgat = nc.dram_tensor("dbgat", [L, DM], BF16, kind=_skind).ap() if DEBUG_STASH else None

    ident_src = nc.inline_tensor(np.eye(128, dtype=np.float32), name="ident_src")
    _lc = np.zeros((128, 24), np.float32); _lc[:, 4:12] = 1.0
    lncol_src = nc.inline_tensor(_lc, name="lncol_src")
    eps_src = nc.inline_tensor(np.full((128, 1), 1e-5, np.float32), name="eps_src")

    prog = {"sync": [], "gpsimd": [], "vector": [], "scalar": [], "tensor": []}

    def w(eng, tok, val):
        if val and val > 0:
            prog[eng].append(lambda e, t=tok, v=val: e.wait_ge(t.h, v))

    def stage(eng, fn, waits=(), inc=None, inck=1):
        for (t, v) in waits:
            w(eng, t, v)
        val = inc.take(inck) if inc is not None else None
        def wrapped(e, fn=fn, inc=inc, inck=inck):
            last = fn(e)
            if inc is not None:
                last.then_inc(inc.h, inck)
        prog[eng].append(wrapped)
        return val

    with contextlib.ExitStack() as ST:
        E = ST.enter_context

        s_ld = Tok(ST, nc, "s_ld")
        s_st = Tok(ST, nc, "s_st")
        s_act = Tok(ST, nc, "s_act")
        s_dve = Tok(ST, nc, "s_dve")
        s_pe = Tok(ST, nc, "s_pe")
        s_out = Tok(ST, nc, "s_out")
        s_gp = Tok(ST, nc, "s_gp")
        s_ph = Tok(ST, nc, "s_ph")

        ident = E(nc.sbuf_tensor("ident", [128, 128], BF16))
        ident32 = E(nc.sbuf_tensor("ident32", [128, 128], F32))
        xpT = {b: E(nc.sbuf_tensor(f"xpT{b}", [128, NG * 96], F32)) for b in "fb"}
        dtwT = {b: E(nc.sbuf_tensor(f"dtwT{b}", [RNK, DI], F32R)) for b in "fb"}
        Atile = {b: E(nc.sbuf_tensor(f"At{b}", [128, NG * NST], F32)) for b in "fb"}
        convw = {b: E(nc.sbuf_tensor(f"cw{b}", [128, NG * K], F32)) for b in "fb"}
        convb = {b: E(nc.sbuf_tensor(f"cb{b}", [128, NG], F32)) for b in "fb"}
        dtb = {b: E(nc.sbuf_tensor(f"dtb{b}", [128, NG], F32)) for b in "fb"}
        Dv = {b: E(nc.sbuf_tensor(f"Dv{b}", [128, NG], F32)) for b in "fb"}
        carry = {b: E(nc.sbuf_tensor(f"carry{b}", [128, NG * NST], F32)) for b in "fb"}

        xsl = E(nc.sbuf_tensor("xsl", [128, NG * CW], F32))
        dtsl = E(nc.sbuf_tensor("dtsl", [128, NG * TC], F32))
        szsl = E(nc.sbuf_tensor("szsl", [128, NG * TC], BF16))
        Bsl = E(nc.sbuf_tensor("Bsl", [128, NST * TC], BF16))
        Csl = E(nc.sbuf_tensor("Csl", [128, NST * TC], BF16))
        dtpre = E(nc.sbuf_tensor("dtpre", [RNK, TC], F32R))
        bcrow = E(nc.sbuf_tensor("bcrow", [128, TC], BF16))
        wt = E(nc.sbuf_tensor("wt", [128, TC], F32))
        conv_acc = E(nc.sbuf_tensor("conv_acc", [128, TC], F32))
        halo = E(nc.sbuf_tensor("halo", [128, NG * 3], F32))
        epsb = E(nc.sbuf_tensor("epsb", [128, 1], F32))
        lncol0 = E(nc.sbuf_tensor("lncol0", [128, 24], F32))
        etmp = E(nc.sbuf_tensor("etmp", [128, TC], F32))
        dasl = [E(nc.sbuf_tensor(f"dasl{i}", [128, NQ * TP2], F32)) for i in range(2)]
        usl = [E(nc.sbuf_tensor(f"usl{i}", [128, NQ * TP2], F32)) for i in range(2)]
        hsl = [E(nc.sbuf_tensor(f"hsl{i}", [128, NQ * TP2], F32)) for i in range(2)]
        htsl = [E(nc.sbuf_tensor(f"htsl{i}", [128, NQ * TC], BF16)) for i in range(2)]
        ysl = [E(nc.sbuf_tensor(f"ysl{i}", [128, TC], BF16)) for i in range(2)]

        pb = [E(nc.psum_tensor(f"pb{i}", [128, 512], F32)) for i in range(8)]

        tcst = 0
        for b in "fb":
            tcst = stage("sync", lambda e, b=b: e.dma_start(out=Atile[b][:], in_=d_Alog[b]), inc=s_ld, inck=16)
            tcst = stage("sync", lambda e, b=b: e.dma_start(out=convw[b][:], in_=d_convw[b]), inc=s_ld, inck=16)
            tcst = stage("sync", lambda e, b=b: e.dma_start(out=convb[b][:], in_=d_convb[b]), inc=s_ld, inck=16)
            tcst = stage("sync", lambda e, b=b: e.dma_start(out=dtb[b][:], in_=d_dtb[b]), inc=s_ld, inck=16)
            tcst = stage("sync", lambda e, b=b: e.dma_start(out=Dv[b][:], in_=d_D[b]), inc=s_ld, inck=16)
            tcst = stage("sync", lambda e, b=b: e.dma_start(out=xpT[b][:], in_=d_xpT[b]), inc=s_ld, inck=16)
        tid = stage("sync", lambda e: e.dma_start(out=dtsl[:, 0:128], in_=ident_src.ap()), inc=s_ld, inck=16)
        t_lc = stage("sync", lambda e: e.dma_start(out=lncol0[:], in_=lncol_src.ap()), inc=s_ld, inck=16)
        t_eps = stage("sync", lambda e: e.dma_start(out=epsb[:], in_=eps_src.ap()), inc=s_ld, inck=16)

        tid32 = stage("sync", lambda e: e.dma_start(out=ident32[:], in_=ident_src.ap()), inc=s_ld, inck=16)
        t_ident = stage("vector", lambda e: e.tensor_copy(out=ident[:], in_=dtsl[:, 0:128]),
                        waits=[(s_ld, tid)], inc=s_dve)

        t_round = t_ident
        for b in "fb":
            t1 = stage("sync",
                       lambda e, b=b: e.dma_start(out=dtsl[0:RNK, 0:DI], in_=d_dtwT[b]),
                       waits=[(s_dve, t_round)], inc=s_ld, inck=16)
            t_round = stage("vector",
                            lambda e, b=b: e.tensor_copy(out=dtwT[b][:], in_=dtsl[0:RNK, 0:DI]),
                            waits=[(s_ld, t1)], inc=s_dve)

        def _aexp(e):
            e.activation(out=Atile["f"][:], in_=Atile["f"][:], func=ACTF.Exp)
            return e.activation(out=Atile["b"][:], in_=Atile["b"][:], func=ACTF.Exp)
        t_aexp = stage("scalar", _aexp, waits=[(s_ld, tcst)], inc=s_act)

        def _aneg(e):
            e.tensor_scalar(out=Atile["f"][:], in0=Atile["f"][:], scalar1=-1.0, scalar2=None, op0=OP.mult)
            return e.tensor_scalar(out=Atile["b"][:], in0=Atile["b"][:], scalar1=-1.0, scalar2=None, op0=OP.mult)
        t_aneg = stage("vector", _aneg, waits=[(s_act, t_aexp)], inc=s_dve)

        def _init(e):
            for i in range(2):
                e.memset(hsl[i][:], 0.0)
                e.memset(dasl[i][:], 0.0)
                e.memset(usl[i][:], 0.0)
            e.memset(carry["f"][:], 0.0)
            e.memset(carry["b"][:], 0.0)
            e.memset(halo[:], 0.0)
            return e.memset(xsl[:], 0.0)
        t_init = stage("vector", _init, inc=s_dve)

        stA = contextlib.ExitStack()
        WinT16 = stA.enter_context(nc.sbuf_tensor("WinT16", [128, MG * 2 * DI], BF16))

        t_wincast = 0
        QW = 1024
        with nc.sbuf_tensor("wstage", [128, QW], F32) as wstage:
            for mi in range(MG):
                for qw in range(2 * DI // QW):
                    t1 = stage("sync",
                               lambda e, mi=mi, qw=qw: e.dma_start(
                                   out=wstage[:], in_=d_WinT[mi * 128:(mi + 1) * 128, qw * QW:(qw + 1) * QW]),
                               waits=[(s_dve, t_wincast)], inc=s_ld, inck=16)
                    t_wincast = stage(
                        "vector",
                        lambda e, mi=mi, qw=qw: e.tensor_copy(
                            out=WinT16[:, mi * 2 * DI + qw * QW:mi * 2 * DI + (qw + 1) * QW], in_=wstage[:]),
                        waits=[(s_ld, t1)], inc=s_dve)

        if DEBUG_STASH:
            stage("gpsimd", lambda e: e.dma_start(out=d_dbgw, in_=WinT16[:]),
                  waits=[(s_dve, t_wincast)], inc=s_st, inck=16)
        u32 = [stA.enter_context(nc.sbuf_tensor("u32_0", [128, DM], F32))] * 2
        uT16 = [stA.enter_context(nc.sbuf_tensor(f"uT16_{i}", [128, MG * TC], BF16)) for i in range(2)]

        state = {
            "margin": 0,
            "xproj_prev": 0,
            "dtmm_prev": 0,
            "scan_prev": 0,
            "bcw_prev": 0,
            "bc_read": 0,
            "slab_free": [0, 0],
            "da_prev": [0, 0],
            "ypsum_free": [0, 0],
            "ysl_free": [0, 0],
            "eplg": [0] * NG,
            "xc": [0] * NG,
            "dtg": [0] * NG,
            "uT_free": [0, 0],
            "u32_free": [t_wincast, t_wincast],
            "u32_pe": [0, 0],
            "tpb_free": [0, 0],
            "outp_prev": 0,
            "yfw_ready": 0,
            "ybw": [0] * NG,
        }

        def emit_mixer_chunk(br, ph, kk, tc):
            DOF = 3

            for g in range(NG):
                xw = xsl[:, g * CW:(g + 1) * CW]

                def _conv(e, g=g, xw=xw):
                    if ph == 0:
                        e.tensor_copy(out=xw[:, 0:3], in_=halo[:, g * 3:(g + 1) * 3])
                        e.tensor_copy(out=halo[:, g * 3:(g + 1) * 3], in_=xw[:, TC:TC + 3])
                    else:
                        e.tensor_copy(out=xw[:, TC + 3:TC + 6], in_=halo[:, g * 3:(g + 1) * 3])
                        e.tensor_copy(out=halo[:, g * 3:(g + 1) * 3], in_=xw[:, 3:6])
                    offs = [(k, (k if ph == 0 else 6 - k)) for k in (3, 2, 1, 0)]
                    k0, o0 = offs[0]
                    e.tensor_scalar(out=conv_acc[:], in0=xw[:, o0:o0 + TC],
                                    scalar1=convw[br][:, g * K + k0:g * K + k0 + 1],
                                    scalar2=None, op0=OP.mult)
                    for k, o in offs[1:]:
                        last = e.scalar_tensor_tensor(out=conv_acc[:], in0=xw[:, o:o + TC],
                                                      scalar=convw[br][:, g * K + k:g * K + k + 1],
                                                      in1=conv_acc[:], op0=OP.mult, op1=OP.add)
                    return last
                prev_silu = state["xc"][g - 1] if g > 0 else state["xc"][NG - 1]
                t_conv = stage("vector", _conv,
                               waits=[(s_act, state["xin_ready"][g]), (s_act, prev_silu)],
                               inc=s_dve)
                state["xc"][g] = stage(
                    "scalar",
                    lambda e, g=g: e.activation(out=xsl[:, g * CW + DOF:g * CW + DOF + TC],
                                                in_=conv_acc[:], func=ACTF.Silu,
                                                bias=convb[br][:, g:g + 1]),
                    waits=[(s_dve, t_conv), (s_st, state["xstash"][g])], inc=s_act)

            for g in range(NG):
                waits = [(s_act, state["xc"][g])]
                if g == 0:
                    waits.append((s_act, state["bc_evac"]))
                    waits.append((s_dve, state["pb2_free"]))
                t_xp = stage(
                    "tensor",
                    lambda e, g=g: e.matmul(out=pb[2][0:96, 0:TC],
                                            lhsT=xpT[br][:, g * 96:(g + 1) * 96],
                                            rhs=xsl[:, g * CW + DOF:g * CW + DOF + TC],
                                            start=(g == 0), stop=(g == NG - 1)),
                    waits=waits, inc=(s_pe if g == NG - 1 else None))
            state["xproj_prev"] = t_xp

            def _bcevac(e):
                e.activation(out=dtpre[:], in_=pb[2][0:RNK, 0:TC], func=ACTF.Copy)
                return e.activation(out=bcrow[RNK:96, :], in_=pb[2][RNK:96, 0:TC], func=ACTF.Copy)
            state["bc_evac"] = stage("scalar", _bcevac,
                                     waits=[(s_pe, t_xp), (s_st, state["bcw_prev"]),
                                            (s_pe, state["dtmm_prev"])],
                                     inc=s_act)
            t_dtpre = state["bc_evac"]

            state["bcw_prev"] = stage(
                "gpsimd",
                lambda e, tc=tc, ph=ph: e.dma_start(out=s_bc[:, (ph * NTC + tc) % (L // TC) * TC:
                                                             ((ph * NTC + tc) % (L // TC) + 1) * TC],
                                                    in_=bcrow[RNK:96, :]),
                waits=[(s_act, state["bc_evac"])], inc=s_st, inck=16)

            def _bcread(e, tc=tc, ph=ph):
                col = (ph * NTC + tc) % (L // TC)
                src = s_bc[:, col * TC:(col + 1) * TC]
                bsrc = src[0:NST, :].unsqueeze(0).to_broadcast([128, NST, TC])
                csrc = src[NST:32, :].unsqueeze(0).to_broadcast([128, NST, TC])
                e.dma_start(out=Bsl[:].rearrange("p (n t) -> p n t", n=NST), in_=bsrc)
                return e.dma_start(out=Csl[:].rearrange("p (n t) -> p n t", n=NST), in_=csrc)
            w("sync", s_st, state["bcw_prev"])
            w("sync", s_dve, state["scan_prev"])
            w("sync", s_gp, s_gp.n)
            _v = s_ld.take(32)
            def _bcread2(e, tc=tc, ph=ph):
                col = (ph * NTC + tc) % (L // TC)
                src = s_bc[:, col * TC:(col + 1) * TC]
                bsrc = src[0:NST, :].unsqueeze(0).to_broadcast([128, NST, TC])
                csrc = src[NST:32, :].unsqueeze(0).to_broadcast([128, NST, TC])
                e.dma_start(out=Bsl[:].rearrange("p (n t) -> p n t", n=NST), in_=bsrc).then_inc(s_ld.h, 16)
                e.dma_start(out=Csl[:].rearrange("p (n t) -> p n t", n=NST), in_=csrc).then_inc(s_ld.h, 16)
            prog["sync"].append(_bcread2)
            state["bc_read"] = _v

            for g in range(NG):
                mmw = [(s_act, t_dtpre if g == 0 else state["dt_evac"]),
                       (s_act, state["dtg"][g])]
                if g == 0:
                    mmw.append((s_dve, state["pb3_free"]))
                t_mm = stage(
                    "tensor",
                    lambda e, g=g: e.matmul(out=pb[3][:, 0:TC],
                                            lhsT=dtwT[br][:, g * 128:(g + 1) * 128],
                                            rhs=dtpre[:], start=True, stop=True),
                    waits=mmw, inc=s_pe)

                def _dt(e, g=g):
                    e.activation(out=etmp[:], in_=pb[3][:, 0:TC], func=ACTF.Exp,
                                 bias=dtb[br][:, g:g + 1])
                    return e.activation(out=dtsl[:, g * TC:(g + 1) * TC], in_=etmp[:],
                                        func=ACTF.Ln, bias=1.0)
                state["dtg"][g] = stage("scalar", _dt,
                                        waits=[(s_pe, t_mm), (s_dve, state["dt_used"][g]),
                                               (s_st, state.get("dbg_tok", 0))],
                                        inc=s_act)
                state["dt_evac"] = state["dtg"][g]
            state["dtmm_prev"] = t_mm

            units = [(g, q) for g in range(NG) for q in range(NU)]
            NUNITS = len(units)
            rot_of = lambda ui: ui % 2

            def emit_w(g):
                stage("vector",
                      lambda e, g=g: e.tensor_tensor(
                          out=wt[:], in0=dtsl[:, g * TC:(g + 1) * TC],
                          in1=xsl[:, g * CW + DOF:g * CW + DOF + TC],
                          op=OP.mult),
                      waits=[(s_act, state["dtg"][g]), (s_gp, s_gp.n)], inc=s_dve)
                state["wt_tok"] = s_dve.n

            def emit_arg(ui):
                g, q = units[ui]
                rot = rot_of(ui)
                def _arg(e, g=g, q=q, rot=rot):
                    dv = dasl[rot][:].rearrange("p (n t) -> p n t", n=NQ)
                    e.memzero(dv[:, :, 0:1])
                    e.memzero(dv[:, :, TP2 - 1:TP2])
                    for j in range(NQ):
                        n = q * NQ + j
                        last = e.activation(
                            out=dasl[rot][:, j * TP2 + 1:j * TP2 + 1 + TC],
                            in_=dtsl[:, g * TC:(g + 1) * TC], func=ACTF.Identity,
                            scale=Atile[br][:, g * NST + n:g * NST + n + 1])
                    return last
                waits = [(s_dve, state["scan_tok"][rot])]
                state["arg_tok"][rot] = stage("scalar", _arg, waits=waits, inc=s_act)

            def emit_exp(ui):
                rot = rot_of(ui)
                state["exp_tok"][rot] = stage(
                    "scalar",
                    lambda e, rot=rot: e.activation(out=dasl[rot][:], in_=dasl[rot][:],
                                                    func=ACTF.Exp),
                    waits=[], inc=s_act)

            def emit_u(ui):
                g, q = units[ui]
                rot = rot_of(ui)
                def _u(e, g=g, q=q, rot=rot, ph=ph):
                    pfx = 0 if ph == 0 else TP2 - 1
                    e.tensor_copy(
                        out=usl[rot][:].rearrange("p (n t) -> p n t", n=NQ)[:, :, pfx:pfx + 1],
                        in_=carry[br][:, g * NST + q * NQ:g * NST + (q + 1) * NQ]
                            .unsqueeze(2))
                    wv = wt[:].unsqueeze(1).to_broadcast([128, NQ, TC])
                    return e.tensor_tensor(
                        out=usl[rot][:].rearrange("p (n t) -> p n t", n=NQ)[:, :, 1:1 + TC],
                        in0=wv,
                        in1=Bsl[:].rearrange("p (n t) -> p n t", n=NST)[:, q * NQ:(q + 1) * NQ, :],
                        op=OP.mult)
                waits = [(s_dve, state["usl_free"][rot]), (s_ld, state["bc_read"]),
                         (s_dve, state["wt_tok"]),
                         (s_dve, state["carry_tok"][(br, g, q)])]
                state["u_tok"][rot] = stage("gpsimd", _u, waits=waits, inc=s_gp)

            def emit_carrysave(ui):
                def _cs(e, ui=ui, ph=ph):
                    pg, pq = units[ui]
                    cs = TC if ph == 0 else 1
                    return e.tensor_copy(
                        out=carry[br][:, pg * NST + pq * NQ:pg * NST + (pq + 1) * NQ]
                            .unsqueeze(2),
                        in_=hsl[rot_of(ui)][:].rearrange("p (n t) -> p n t", n=NQ)[:, :, cs:cs + 1])
                pg, pq = units[ui]
                state["carry_tok"][(br, pg, pq)] = stage("vector", _cs, inc=s_dve)

            def emit_scan(ui):
                g, q = units[ui]
                rot = rot_of(ui)
                def _scan(e, g=g, q=q, rot=rot, ph=ph):
                    if ph == 0:
                        last = e.tensor_tensor_scan(
                            out=hsl[rot][:],
                            data0=dasl[rot][:],
                            data1=usl[rot][:],
                            initial=0.0, op0=OP.mult, op1=OP.add)
                    else:
                        last = e.tensor_tensor_scan(
                            out=hsl[rot][:][:, ::-1],
                            data0=dasl[rot][:][:, ::-1],
                            data1=usl[rot][:][:, ::-1],
                            initial=0.0, op0=OP.mult, op1=OP.add)
                    return last
                waits = [(s_act, state["exp_tok"][rot]), (s_gp, state["u_tok"][rot]),
                         (s_pe, state["slab_free"][rot])]
                state["scan_tok"][rot] = stage("vector", _scan, waits=waits, inc=s_dve)
                state["usl_free"][rot] = state["scan_tok"][rot]

            def emit_ctt(ui):
                g, q = units[ui]
                rot = rot_of(ui)
                def _c(e, g=g, q=q, rot=rot):
                    return e.tensor_tensor(
                        out=htsl[rot][:].rearrange("p (n t) -> p n t", n=NQ),
                        in0=hsl[rot][:].rearrange("p (n t) -> p n t", n=NQ)[:, :, 1:1 + TC],
                        in1=Csl[:].rearrange("p (n t) -> p n t", n=NST)[:, q * NQ:(q + 1) * NQ, :],
                        op=OP.mult)
                waits = [(s_pe, state["slab_free"][rot])]
                state["ctt_tok"][rot] = stage("vector", _c, waits=waits, inc=s_dve)

            def emit_acc(ui):
                g, q = units[ui]
                rot = rot_of(ui)
                ybank = 4 + (g % 2)
                waits = [(s_dve, state["ctt_tok"][rot])]
                if q == 0:
                    waits += [(s_dve, state["ypsum_free"][g % 2]), (s_dve, t_ident)]
                def _acc(e, g=g, q=q, rot=rot, ybank=ybank):
                    for j in range(NQ):
                        last = e.matmul(out=pb[ybank][:, 0:TC], lhsT=ident[:],
                                        rhs=htsl[rot][:, j * TC:(j + 1) * TC],
                                        start=(q == 0 and j == 0),
                                        stop=(q == NU - 1 and j == NQ - 1))
                    return last
                state["slab_free"][rot] = stage("tensor", _acc, waits=waits, inc=s_pe)
                return state["slab_free"][rot]

            def emit_epilogue(g, t_yacc):
                yrot = g % 2
                if ph == 0:
                    ytarget = ysl[yrot][:]
                    ywaits = [(s_st, state["ysl_free"][yrot])]
                else:
                    ytarget = ybw[:, g * TC:(g + 1) * TC]
                    ywaits = [(s_pe, state["outp_prev"]), (s_st, state.get("dbg_tok", 0))]

                def _eplg(e, g=g, yrot=yrot, ytarget=ytarget):
                    e.scalar_tensor_tensor(
                        out=conv_acc[:],
                        in0=xsl[:, g * CW + DOF:g * CW + DOF + TC],
                        scalar=Dv[br][:, g:g + 1],
                        in1=pb[4 + yrot][:, 0:TC], op0=OP.mult, op1=OP.add)
                    return e.tensor_tensor(out=ytarget, in0=conv_acc[:],
                                           in1=szsl[:, g * TC:(g + 1) * TC], op=OP.mult)
                t_ep = stage("vector", _eplg,
                             waits=[(s_pe, t_yacc), (s_act, state["sz_ready"][g]),
                                    (s_act, state["xc"][NG - 1])] + ywaits,
                             inc=s_dve)
                state["eplg"][g] = t_ep
                state["ypsum_free"][yrot] = t_ep
                state["dt_used"][g] = t_ep
                if ph == 0:
                    state["ysl_free"][yrot] = stage(
                        "gpsimd",
                        lambda e, g=g, kk=kk, yrot=yrot: e.dma_start(
                            out=s_yfwd[g * 128:(g + 1) * 128, kk * TC:(kk + 1) * TC],
                            in_=ysl[yrot][:]),
                        waits=[(s_dve, t_ep)], inc=s_st, inck=16)
                else:
                    state["ybw"][g] = t_ep

            emit_w(0)
            emit_arg(0)
            emit_exp(0)
            emit_u(0)
            for ui in range(NUNITS):
                g, q = units[ui]
                if ui >= 2:
                    emit_carrysave(ui - 2)
                if ui + 1 < NUNITS:
                    gn, qn = units[ui + 1]
                    if qn == 0:
                        emit_w(gn)
                    emit_arg(ui + 1)
                    emit_u(ui + 1)
                emit_scan(ui)
                if ui + 1 < NUNITS:
                    emit_exp(ui + 1)
                emit_ctt(ui)
                t_yacc = emit_acc(ui)
                if q == NU - 1:
                    emit_epilogue(g, t_yacc)
            emit_carrysave(NUNITS - 2)
            stage("vector",
                  lambda e: e.tensor_copy(out=conv_acc[:],
                                          in_=hsl[rot_of(NUNITS - 1)][:, 0:TC]),
                  inc=s_dve)
            emit_carrysave(NUNITS - 1)
            state["scan_prev"] = state["scan_tok"][rot_of(NUNITS - 1)]

        state["bc_evac"] = 0
        state["carry_tok"] = {("f", g, q): 0 for g in range(NG) for q in range(NU)}
        state["carry_tok"].update({("b", g, q): 0 for g in range(NG) for q in range(NU)})
        state["wt_tok"] = 0
        state["wt_gp"] = 0
        state["arg_tok"] = [0, 0]
        state["exp_tok"] = [0, 0]
        state["u_tok"] = [0, 0]
        state["scan_tok"] = [0, 0]
        state["ctt_tok"] = [0, 0]
        state["usl_free"] = [0, 0]
        state["dt_evac"] = 0
        state["dt_used"] = [0] * NG
        state["xin_ready"] = [0] * NG
        state["sz_ready"] = [0] * NG
        state["xstash"] = [0] * NG
        state["pb2_free"] = 0
        state["dbg_tok"] = 0
        state["pb3_free"] = t_round
        state["szstash_prev"] = 0

        for tc in range(NTC):
            kk = tc
            rot = tc % 2

            if tc == 0:
                w("scalar", s_dve, t_init)

            for ts in range(TSUB):
                i = (tc * TSUB + ts) % 2
                t_load = stage("sync",
                               lambda e, i=i, kk=kk, ts=ts: e.dma_start(
                                   out=u32[i][:], in_=d_u[kk * TC + ts * 128:kk * TC + (ts + 1) * 128, :]),
                               waits=[(s_dve, state["u32_free"][i]), (s_pe, state["u32_pe"][i])],
                               inc=s_ld, inck=16)
                state["u32_free"][i] = 0
                for half in range(2):
                    def _tp(e, i=i, half=half):
                        for mm in range(4):
                            m = half * 4 + mm
                            last = e.matmul(out=pb[6 + half][:, mm * 128:(mm + 1) * 128],
                                            lhsT=u32[i][:, m * 128:(m + 1) * 128],
                                            rhs=ident32[:], is_transpose=True,
                                            start=(mm == 0), stop=(mm == 3))
                        return last
                    t_tp = stage("tensor", _tp,
                                 waits=[(s_ld, t_load), (s_act, state["tpb_free"][half])],
                                 inc=s_pe)
                    state["u32_pe"][0] = t_tp
                    state["u32_pe"][1] = t_tp
                    def _ev(e, rot=rot, ts=ts, half=half):
                        for mm in range(4):
                            m = half * 4 + mm
                            last = e.activation(
                                out=uT16[rot][:, m * TC + ts * 128:m * TC + ts * 128 + 128],
                                in_=pb[6 + half][:, mm * 128:(mm + 1) * 128], func=ACTF.Copy)
                        return last
                    state["tpb_free"][half] = stage("scalar", _ev, waits=[(s_pe, t_tp)], inc=s_act)
            t_uT = state["tpb_free"][1]

            t_evac_po = [0] * 32
            for po in range(32):
                bank = po % 2
                waits = [(s_act, t_uT)]
                if po >= 2:
                    waits.append((s_act, t_evac_po[po - 2]))
                if po == 0:
                    waits.append((s_dve, t_wincast))
                def _mm(e, po=po, bank=bank, rot=rot):
                    for mi in range(MG):
                        last = e.matmul(out=pb[bank][:, 0:TC],
                                        lhsT=WinT16[:, mi * 2 * DI + po * 128:mi * 2 * DI + (po + 1) * 128],
                                        rhs=uT16[rot][:, mi * TC:(mi + 1) * TC],
                                        start=(mi == 0), stop=(mi == MG - 1))
                    return last
                t_mm = stage("tensor", _mm, waits=waits, inc=s_pe)
                if po < NG:
                    g = po
                    state["xin_ready"][g] = stage(
                        "scalar",
                        lambda e, g=g, bank=bank: e.activation(
                            out=xsl[:, g * CW + 3:g * CW + 3 + TC],
                            in_=pb[bank][:, 0:TC], func=ACTF.Copy),
                        waits=[(s_pe, t_mm), (s_dve, state["eplg"][g]),
                               (s_pe, state["xproj_prev"]),
                               (s_st, state.get("dbg_tok", 0))],
                        inc=s_act)
                    t_evac_po[po] = state["xin_ready"][g]
                else:
                    g = po - NG
                    state["sz_ready"][g] = stage(
                        "scalar",
                        lambda e, g=g, bank=bank: e.activation(
                            out=szsl[:, g * TC:(g + 1) * TC],
                            in_=pb[bank][:, 0:TC], func=ACTF.Silu),
                        waits=[(s_pe, t_mm), (s_dve, state["eplg"][g]),
                               (s_st, state["szstash_prev"])],
                        inc=s_act)
                    t_evac_po[po] = state["sz_ready"][g]
            state["uT_free"][rot] = t_mm

            for g in range(NG):
                state["xstash"][g] = stage(
                    "gpsimd",
                    lambda e, g=g, kk=kk: e.dma_start(
                        out=s_xin[g * 128:(g + 1) * 128, kk * TC:(kk + 1) * TC],
                        in_=xsl[:, g * CW + 3:g * CW + 3 + TC]),
                    waits=[(s_act, t_evac_po[g])], inc=s_st, inck=16)
            state["szstash_prev"] = stage(
                "gpsimd",
                lambda e, kk=kk: e.dma_start(
                    out=s_siluz.rearrange("(g p) l -> p g l", p=128)[:, :, kk * TC:(kk + 1) * TC],
                    in_=szsl[:].rearrange("p (g c) -> p g c", g=NG)),
                waits=[(s_act, t_evac_po[31])], inc=s_st, inck=16)

            emit_mixer_chunk("f", 0, kk, tc)
            if DEBUG_STASH:
                stage("gpsimd",
                      lambda e, kk=kk: e.dma_start(
                          out=d_dbgxc.rearrange("(g p) l -> p g l", p=128)[:, :, kk * TC:(kk + 1) * TC],
                          in_=xsl[:].rearrange("p (g c) -> p g c", g=NG)[:, :, 3:3 + TC]),
                      waits=[(s_dve, state["eplg"][NG - 1])], inc=s_st, inck=16)
                state["dbg_tok"] = stage("gpsimd",
                      lambda e, kk=kk: e.dma_start(
                          out=d_dbgdt.rearrange("(g p) l -> p g l", p=128)[:, :, kk * TC:(kk + 1) * TC],
                          in_=dtsl[:].rearrange("p (g c) -> p g c", g=NG)),
                      waits=[(s_dve, state["eplg"][NG - 1])], inc=s_st, inck=16)

        stA.close()
        w("sync", s_ld, s_ld.n)
        w("gpsimd", s_st, s_st.n)
        for eng in ["sync", "gpsimd", "vector", "scalar", "tensor"]:
            s_ph.take(1)
            def _barrier(e):
                e.nop().then_inc(s_ph.h, 1)
            prog[eng].append(_barrier)
        for eng in ["sync", "gpsimd", "vector", "scalar", "tensor"]:
            w(eng, s_ph, 5)

        stB = contextlib.ExitStack()
        ybw = stB.enter_context(nc.sbuf_tensor("ybw", [128, NG * TC], BF16))
        yfw = stB.enter_context(nc.sbuf_tensor("yfw", [128, NG * TC], BF16))
        WoutT16 = stB.enter_context(nc.sbuf_tensor("WoutT16", [128, NG * DM], BF16))
        WfT16 = stB.enter_context(nc.sbuf_tensor("WfT16", [128, MG * DM], BF16))
        ures = stB.enter_context(nc.sbuf_tensor("ures", [128, DM], F32))
        x1 = stB.enter_context(nc.sbuf_tensor("x1", [128, DM], F32))
        x1T = stB.enter_context(nc.sbuf_tensor("x1T", [128, MG * 128], BF16))
        outsl = stB.enter_context(nc.sbuf_tensor("outsl", [128, DM], F32))
        lncol = lncol0
        wstageB = ures

        t_wc = 0
        for g in range(NG):
            t1 = stage("sync", lambda e, g=g: e.dma_start(out=wstageB[:], in_=d_WoutT[g * 128:(g + 1) * 128, :]),
                       waits=[(s_dve, t_wc)], inc=s_ld, inck=16)
            t_wc = stage("vector",
                         lambda e, g=g: e.tensor_copy(out=WoutT16[:, g * DM:(g + 1) * DM], in_=wstageB[:]),
                         waits=[(s_ld, t1)], inc=s_dve)
        t_ws = 0
        for m in range(MG):
            t1 = stage("sync", lambda e, m=m: e.dma_start(out=wstageB[:], in_=d_WfT[m * 128:(m + 1) * 128, :]),
                       waits=[(s_act, t_ws), (s_dve, t_wc if m == 0 else 0)], inc=s_ld, inck=16)
            t_ws = stage("scalar",
                         lambda e, m=m: e.activation(out=WfT16[:, m * DM:(m + 1) * DM], in_=wstageB[:],
                                                     func=ACTF.Sign),
                         waits=[(s_ld, t1)], inc=s_act)
        t_wB = t_ws

        pm = {
            "ures_free": 0, "x1T_free": [0, 0], "attn_free": [0] * 4,
            "ffn_free": [0, 0], "out_free": 0, "tp_free": [0, 0],
            "yfw_free": 0, "lncol_free": 0,
        }

        def emit_ln(buf, tok_data):
            def _a1(e):
                e.activation(out=ures[:], in_=buf[:], func=ACTF.Copy, accum_out=lncol[:, 0:1])
                e.activation(out=ures[:], in_=buf[:], func=ACTF.Square, accum_out=lncol[:, 2:3])
                e.activation(out=lncol[:, 1:2], in_=lncol[:, 0:1], func=ACTF.Copy, scale=1.0 / DM)
                return e.activation(out=lncol[:, 12:13], in_=lncol[:, 0:1], func=ACTF.Square,
                                    scale=1.0 / DM)
            t_a1 = stage("scalar", _a1, waits=[(s_dve, tok_data)], inc=s_act)
            t_v1 = stage("vector",
                         lambda e: e.scalar_tensor_tensor(out=lncol[:, 13:14], in0=epsb[:],
                                                          scalar=lncol[:, 12:13], in1=epsb[:],
                                                          op0=OP.subtract, op1=OP.bypass),
                         waits=[(s_act, t_a1)], inc=s_dve)
            t_a2 = stage("scalar",
                         lambda e: e.activation(out=lncol[:, 3:4], in_=lncol[:, 2:3],
                                                func=ACTF.Sqrt, scale=1.0 / DM,
                                                bias=lncol[:, 13:14]),
                         waits=[(s_dve, t_v1)], inc=s_act)
            t_v2 = stage("vector",
                         lambda e: e.reciprocal(out=lncol[:, 16:24], in_=lncol[:, 3:11]),
                         waits=[(s_act, t_a2)], inc=s_dve)
            t_a3 = stage("scalar",
                         lambda e: e.activation(out=lncol[:, 15:16], in_=lncol[:, 16:17],
                                                func=ACTF.Copy),
                         waits=[(s_dve, t_v2)], inc=s_act)
            t_ap = stage("vector",
                         lambda e: e.scalar_tensor_tensor(
                             out=buf[:], in0=buf[:], scalar=lncol[:, 1:2],
                             in1=lncol[:, 16:17].to_broadcast([128, DM]),
                             op0=OP.subtract, op1=OP.mult),
                         waits=[(s_act, t_a3)], inc=s_dve)
            return t_ap

        def emit_postmixer(kk):
            t_yf = stage("sync",
                         lambda e, kk=kk: e.dma_start(
                             out=yfw[:].rearrange("p (g c) -> p g c", g=NG),
                             in_=s_yfwd.rearrange("(g p) l -> p g l", p=128)[:, :, kk * TC:(kk + 1) * TC]),
                         waits=[(s_pe, pm["yfw_free"])], inc=s_ld, inck=16)
            for ts in range(TSUB):
                t_u = stage("sync",
                            lambda e, kk=kk, ts=ts: e.dma_start(
                                out=ures[:], in_=d_u[kk * TC + ts * 128:kk * TC + (ts + 1) * 128, :]),
                            waits=[(s_dve, pm["ures_free"]), (s_act, t_wB)], inc=s_ld, inck=16)
                for fo in range(2):
                    bank = fo
                    waits = [(s_ld, t_yf), (s_dve, pm["attn_free"][bank]), (s_act, t_wB)]
                    waits += [(s_dve, state["ybw"][g]) for g in range(NG)] if (ts == 0 and fo == 0) else []
                    def _op(e, ts=ts, fo=fo, bank=bank):
                        for g in range(NG):
                            e.matmul(out=pb[bank][:, 0:512],
                                     lhsT=yfw[:, g * TC + ts * 128:g * TC + ts * 128 + 128],
                                     rhs=WoutT16[:, g * DM + fo * 512:g * DM + fo * 512 + 512],
                                     start=(g == 0), stop=False)
                        for g in range(NG):
                            last = e.matmul(out=pb[bank][:, 0:512],
                                            lhsT=ybw[:, g * TC + ts * 128:g * TC + ts * 128 + 128],
                                            rhs=WoutT16[:, g * DM + fo * 512:g * DM + fo * 512 + 512],
                                            start=False, stop=(g == NG - 1))
                        return last
                    t_at = stage("tensor", _op, waits=waits, inc=s_pe)
                    pm["attn_bank_tok"][bank] = t_at
                pm["outp_tok"] = t_at

                if DEBUG_STASH:
                    def _dbga(e):
                        e.tensor_copy(out=x1t16[:, 0:512], in_=pb[0][:, 0:512])
                        return e.tensor_copy(out=x1t16[:, 512:1024], in_=pb[1][:, 0:512])
                    t_dbga = stage("vector", _dbga,
                          waits=[(s_pe, pm["attn_bank_tok"][0]), (s_pe, pm["attn_bank_tok"][1])],
                          inc=s_dve)
                    pm["dbga_tok"] = stage("gpsimd",
                          lambda e, kk=kk, ts=ts: e.dma_start(
                              out=d_dbgat[kk * TC + ts * 128:kk * TC + (ts + 1) * 128, :],
                              in_=x1t16[:]),
                          waits=[(s_dve, t_dbga)], inc=s_st, inck=16)
                def _res(e, ts=ts):
                    e.tensor_tensor(out=x1[:, 0:512], in0=ures[:, 0:512], in1=pb[0][:, 0:512], op=OP.add)
                    return e.tensor_tensor(out=x1[:, 512:1024], in0=ures[:, 512:1024],
                                           in1=pb[1][:, 0:512], op=OP.add)
                t_x1 = stage("vector", _res,
                             waits=[(s_ld, t_u), (s_pe, pm["attn_bank_tok"][1]),
                                    (s_dve, pm["lncol_free"]), (s_st, pm.get("dbga_tok", 0))],
                             inc=s_dve)

                pm["attn_free"][0] = t_x1
                pm["attn_free"][1] = t_x1

                t_x1n = emit_ln(x1, t_x1)

                if DEBUG_STASH:
                    stage("gpsimd",
                          lambda e, kk=kk, ts=ts: e.dma_start(
                              out=d_dbgx1[kk * TC + ts * 128:kk * TC + (ts + 1) * 128, :],
                              in_=x1[:]),
                          waits=[(s_dve, t_x1n)], inc=s_st, inck=16)
                for half in range(2):
                    def _tp(e, half=half):
                        for mm in range(4):
                            m = half * 4 + mm
                            last = e.matmul(out=pb[6 + half][:, mm * 128:(mm + 1) * 128],
                                            lhsT=x1[:, m * 128:(m + 1) * 128],
                                            rhs=ident32[:], is_transpose=True,
                                            start=(mm == 0), stop=(mm == 3))
                        return last
                    t_tp = stage("tensor", _tp,
                                 waits=[(s_dve, t_x1n), (s_act, pm["tp_free"][half])], inc=s_pe)
                    def _ev(e, half=half):
                        for mm in range(4):
                            m = half * 4 + mm
                            last = e.activation(out=x1T[:, m * 128:(m + 1) * 128],
                                                in_=pb[6 + half][:, mm * 128:(mm + 1) * 128],
                                                func=ACTF.Copy)
                        return last
                    pm["tp_free"][half] = stage("scalar", _ev,
                                                waits=[(s_pe, t_tp), (s_pe, pm["x1T_free"][half])],
                                                inc=s_act)
                t_x1T = pm["tp_free"][1]

                for fo in range(2):
                    def _ffn(e, fo=fo):
                        for m in range(MG):
                            last = e.matmul(out=pb[2 + fo][:, 0:512],
                                            lhsT=x1T[:, m * 128:(m + 1) * 128],
                                            rhs=WfT16[:, m * DM + fo * 512:m * DM + fo * 512 + 512],
                                            start=(m == 0), stop=(m == MG - 1))
                        return last
                    t_ffn = stage("tensor", _ffn,
                                  waits=[(s_act, t_x1T), (s_dve, pm["ffn_free"][fo])], inc=s_pe)
                pm["x1T_free"][0] = t_ffn
                pm["x1T_free"][1] = t_ffn

                def _res2(e):
                    e.tensor_tensor(out=outsl[:, 0:512], in0=x1[:, 0:512], in1=pb[2][:, 0:512], op=OP.add)
                    return e.tensor_tensor(out=outsl[:, 512:1024], in0=x1[:, 512:1024],
                                           in1=pb[3][:, 0:512], op=OP.add)
                t_o1 = stage("vector", _res2, waits=[(s_pe, t_ffn), (s_out, pm["out_free"])], inc=s_dve)
                pm["ffn_free"][0] = t_o1
                pm["ffn_free"][1] = t_o1
                pm["pb2_free"] = t_o1
                pm["pb3_free"] = t_o1

                t_o2 = emit_ln(outsl, t_o1)
                pm["lncol_free"] = t_o2

                pm["out_free"] = stage(
                    "gpsimd",
                    lambda e, kk=kk, ts=ts: e.dma_start(
                        out=d_out[kk * TC + ts * 128:kk * TC + (ts + 1) * 128, :], in_=outsl[:]),
                    waits=[(s_dve, t_o2)], inc=s_out, inck=16)
                pm["ures_free"] = t_o2
            pm["yfw_free"] = pm["outp_tok"]

        pm["attn_bank_tok"] = [0] * 4
        pm["outp_tok"] = 0
        pm["pb2_free"] = 0
        pm["pb3_free"] = 0

        t_hz = stage("vector", lambda e: e.memset(halo[:], 0.0), inc=s_dve)
        for g in range(NG):
            state["xstash"][g] = 0
        state["szstash_prev"] = 0

        for tcB in range(NTC):
            kk = NTC - 1 - tcB

            t_x = stage("sync",
                        lambda e, kk=kk: e.dma_start(
                            out=xsl[:].rearrange("p (g c) -> p g c", g=NG)[:, :, 3:3 + TC],
                            in_=s_xin.rearrange("(g p) l -> p g l", p=128)[:, :, kk * TC:(kk + 1) * TC]),
                        waits=[(s_dve, max(state["eplg"])), (s_pe, state["xproj_prev"])],
                        inc=s_ld, inck=16)
            t_z = stage("sync",
                        lambda e, kk=kk: e.dma_start(
                            out=szsl[:].rearrange("p (g c) -> p g c", g=NG),
                            in_=s_siluz.rearrange("(g p) l -> p g l", p=128)[:, :, kk * TC:(kk + 1) * TC]),
                        waits=[], inc=s_ld, inck=16)
            for g in range(NG):
                state["xin_ready"][g] = 0
                state["sz_ready"][g] = 0
            w("vector", s_ld, t_x)
            w("vector", s_ld, t_z)
            if tcB == 0:
                w("vector", s_dve, t_hz)

            emit_mixer_chunk("b", 1, kk, tcB)
            if DEBUG_STASH:
                state["dbg_tok"] = stage("gpsimd",
                      lambda e, kk=kk: e.dma_start(
                          out=d_dbgyb.rearrange("(g p) l -> p g l", p=128)[:, :, kk * TC:(kk + 1) * TC],
                          in_=ybw[:].rearrange("p (g c) -> p g c", g=NG)),
                      waits=[(s_dve, state["eplg"][NG - 1])], inc=s_st, inck=16)
            emit_postmixer(kk)
            state["outp_prev"] = pm["outp_tok"]
            state["pb2_free"] = pm["pb2_free"]
            state["pb3_free"] = pm["pb3_free"]

        w("gpsimd", s_out, s_out.n)
        prog["gpsimd"].append(lambda e: e.nop())

        with nc.Block() as block:
            @block.sync
            def _(eng):
                for f in prog["sync"]:
                    f(eng)

            @block.gpsimd
            def _(eng):
                for f in prog["gpsimd"]:
                    f(eng)

            @block.vector
            def _(eng):
                for f in prog["vector"]:
                    f(eng)

            @block.scalar
            def _(eng):
                for f in prog["scalar"]:
                    f(eng)

            @block.tensor
            def _(eng):
                for f in prog["tensor"]:
                    f(eng)

        stB.close()

    return nc


def kernel(**inputs):
    xT = lambda a: np.ascontiguousarray(np.asarray(a, np.float32).T)

    def arr128(v):
        return np.ascontiguousarray(np.asarray(v, np.float32).reshape(NG, 128).T)

    def alog(v):
        return np.ascontiguousarray(
            np.transpose(np.asarray(v, np.float32).reshape(NG, 128, NST), (1, 0, 2)).reshape(128, NG * NST))

    def cw_(v):
        return np.ascontiguousarray(
            np.transpose(np.asarray(v, np.float32).reshape(NG, 128, K), (1, 0, 2)).reshape(128, NG * K))

    def xp_(v):
        vT = np.asarray(v, np.float32).T.reshape(NG, 128, 96)
        return np.ascontiguousarray(np.transpose(vT, (1, 0, 2)).reshape(128, NG * 96))

    base = {
        "WinT": xT(inputs["L0_attn_in_proj_w"]),
        "WoutT": xT(inputs["L0_attn_out_proj_w"]),
        "WfT": xT(inputs["L0_ffn_weight"]),
        "xpT_f": xp_(inputs["L0_attn_fwd_x_proj_w"]),
        "xpT_b": xp_(inputs["L0_attn_bwd_x_proj_w"]),
        "dtwT_f": xT(inputs["L0_attn_fwd_dt_w"]),
        "dtwT_b": xT(inputs["L0_attn_bwd_dt_w"]),
        "Alog_f": alog(inputs["L0_attn_fwd_A_log"]),
        "Alog_b": alog(inputs["L0_attn_bwd_A_log"]),
        "convw_f": cw_(inputs["L0_attn_fwd_conv_w"]),
        "convw_b": cw_(inputs["L0_attn_bwd_conv_w"]),
        "convb_f": arr128(inputs["L0_attn_fwd_conv_b"]),
        "convb_b": arr128(inputs["L0_attn_bwd_conv_b"]),
        "dtb_f": arr128(inputs["L0_attn_fwd_dt_b"]),
        "dtb_b": arr128(inputs["L0_attn_bwd_dt_b"]),
        "Dv_f": arr128(inputs["L0_attn_fwd_D"]),
        "Dv_b": arr128(inputs["L0_attn_bwd_D"]),
    }
    u = np.asarray(inputs["L0_attn_u"], np.float32)
    in_maps = [dict(base, u=np.ascontiguousarray(u[b])) for b in range(B)]

    nc = build_program()
    res = run_bass_kernel_spmd(nc, in_maps, core_ids=list(range(B)))
    return np.stack([np.asarray(res.results[b]["out"], np.float32) for b in range(B)])
